# version 09
# baseline (speedup 1.0000x reference)
import numpy as np
from contextlib import ExitStack

import concourse.bass as bass
import concourse.mybir as mybir
from concourse.bass_utils import run_bass_kernel_spmd

F32 = mybir.dt.float32
BF16 = mybir.dt.bfloat16
FP8 = mybir.dt.float8e4
I32 = mybir.dt.int32
AF = mybir.ActivationFunctionType
ALU = mybir.AluOpType
AX = mybir.AxisListType

NCORES = 8
RPG = 4
B, S, D = 2, 4096, 2048
TLOC = 1024
NT = 8
RG = [[0, 1, 2, 3], [4, 5, 6, 7]]
NEG_BIG = -1.0e30
LN_EPS = 1e-5
RMS_EPS = 1.1920929e-07
ROPE_BASE = 10000.0
TWO_PI = float(2.0 * np.pi)
INV_2PI = float(1.0 / TWO_PI)
CW1 = float(np.float32(6.28125))
CW2 = float(np.float32(2.0 * np.pi - 6.28125))
CW3 = float(2.0 * np.pi - 6.28125 - float(np.float32(2.0 * np.pi - 6.28125)))
PIO2_HI = float(np.float32(np.pi / 2))
PIO2_LO = float(np.pi / 2 - float(np.float32(np.pi / 2)))
SCALE_HD = float(1.0 / np.sqrt(128.0))

ENGS = ("sync", "tensor", "vector", "scalar", "gpsimd")


class Ctx:

    def __init__(self, nc):
        self.nc = nc
        self.items = []
        self.sems = {}
        self.cnt = {}
        self.dirty = {e: set() for e in ENGS}
        self._es = None

    def sem(self, name):
        if name not in self.sems:
            h = self._es.enter_context(self.nc.semaphore(name))
            self.sems[name] = h
            self.cnt[name] = 0
        return self.sems[name]

    def _names(self, aps):
        out = []
        for a in aps:
            t = getattr(a, "tensor", None)
            if t is not None:
                out.append(t.name)
        return out

    def op(self, eng, emit, outs=(), ins=(), inc=None, inc_by=1):
        innames = self._names(ins)
        d = self.dirty[eng]
        if any(n in d for n in innames):
            self.items.append((eng, lambda e: e.drain()))
            d.clear()
        for n in self._names(outs):
            d.add(n)
        val = None
        if inc is not None:
            self.sem(inc)
            self.cnt[inc] += inc_by
            val = self.cnt[inc]
            semh = self.sems[inc]

            def fn(e, emit=emit, semh=semh, by=inc_by):
                r = emit(e)
                r.then_inc(semh, by)
            self.items.append((eng, fn))
        else:
            self.items.append((eng, emit))
        return val

    def dma(self, eng, out, in_, inc):
        return self.op(
            eng, lambda e, o=out, i=in_: e.dma_start(out=o, in_=i),
            outs=(), ins=(), inc=inc, inc_by=16)

    def wait(self, eng, sem, val):
        semh = self.sem(sem)
        self.items.append((eng, lambda e, s=semh, v=val: e.wait_ge(s, v)))

    def wait_cur(self, eng, sem):
        self.wait(eng, sem, self.cnt.get(sem, 0))

    def barrier(self, engs=ENGS, sems=None):
        names = sems if sems is not None else list(self.cnt.keys())
        for e in engs:
            for s in names:
                v = self.cnt.get(s, 0)
                if v > 0:
                    self.wait(e, s, v)

    def replay(self, eng_name, engobj):
        for e, fn in self.items:
            if e == eng_name:
                fn(engobj)


def ceil_div(a, b):
    return (a + b - 1) // b



def build_program(upto="E1", taps=()):
    nc = bass.Bass("TRN2", target_bir_lowering=False)
    ph_order = ["PREP", "A0", "B0", "G0", "C0", "D0", "E0",
                "A1", "B1", "G1", "C1", "D1", "E1"]
    assert upto in ph_order
    lim = ph_order.index(upto)

    def on(ph):
        return ph_order.index(ph) <= lim

    def din(name, shape, dt=F32):
        return nc.dram_tensor(name, shape, dt, kind="ExternalInput").ap()

    def dint(name, shape, dt=F32):
        return nc.dram_tensor(name, shape, dt).ap()

    x_in = din("x_in", [TLOC, D])
    qpos_cols = din("qpos_cols", [128, NT])
    qpos_row = din("qpos_row", [1, TLOC])
    w0q_t = din("w0q_t", [D, 2048]); w0k_t = din("w0k_t", [D, 512])
    w0v_t = din("w0v_t", [D, 512]);  w0o_t = din("w0o_t", [D, 2048])
    b0q = din("b0q", [1, 2048]); b0k = din("b0k", [1, 512])
    b0v = din("b0v", [1, 512]);  b0o = din("b0o", [1, 2048])
    gain0 = din("gain0", [128, 16])
    p0g = din("p0g", [256, 6144]); p0u = din("p0u", [256, 6144])
    p0d = din("p0d", [768, 2048])
    gs0 = din("gs0", [128, 48]); us0 = din("us0", [128, 48])
    ds0bc = din("ds0bc", [128, 2048])
    w1q_t = din("w1q_t", [D, 2048]); w1k_t = din("w1k_t", [D, 2048])
    w1v_t = din("w1v_t", [D, 2048]); w1o_t = din("w1o_t", [D, 2048])
    b1q = din("b1q", [1, 2048]); b1k = din("b1k", [1, 2048])
    b1v = din("b1v", [1, 2048]); b1o = din("b1o", [1, 2048])
    p1g = din("p1g", [1024, 2048]); p1u = din("p1u", [1024, 2048])
    p1d = din("p1d", [256, 8192])

    y_out = nc.dram_tensor("y", [TLOC, D], F32, kind="ExternalOutput").ap()

    x0d = dint("x0d", [TLOC, D])
    xnd = dint("xnd", [TLOC, D])
    qd = dint("qd", [16, 128, TLOC])
    gd = dint("gd", [NT, 128, D])
    k0s = dint("k0s", [512, TLOC]); k0g = dint("k0g", [RPG * 512, TLOC])
    v0s = dint("v0s", [TLOC, 512]); v0g = dint("v0g", [RPG * TLOC, 512])
    k1s = dint("k1s", [2048, TLOC]); k1g = dint("k1g", [RPG * 2048, TLOC])
    v1s = dint("v1s", [TLOC, 2048]); v1g = dint("v1g", [RPG * TLOC, 2048])
    q0g_s = dint("q0g_s", [256, 6144], FP8); q0g_g = dint("q0g_g", [2048, 6144], FP8)
    q0u_s = dint("q0u_s", [256, 6144], FP8); q0u_g = dint("q0u_g", [2048, 6144], FP8)
    q0d_s = dint("q0d_s", [768, 2048], FP8); q0d_g = dint("q0d_g", [6144, 2048], FP8)
    dq1g_s = dint("dq1g_s", [2048, 1024]); dq1g_g = dint("dq1g_g", [8 * 2048, 1024])
    dq1u_s = dint("dq1u_s", [2048, 1024]); dq1u_g = dint("dq1u_g", [8 * 2048, 1024])
    dq1d_s = dint("dq1d_s", [8192, 256]);  dq1d_g = dint("dq1d_g", [8 * 8192, 256])

    tap_outs = {}
    dram_by_name = dict(x0d=x0d, xnd=xnd, qd=qd, gd=gd, k0g=k0g, v0g=v0g, k1g=k1g, v1g=v1g,
                        q0g_g=q0g_g, q0d_g=q0d_g, dq1g_g=dq1g_g, dq1d_g=dq1d_g,
                        k0s=k0s, v0s=v0s, k1s=k1s, v1s=v1s, dq1g_s=dq1g_s,
                        dq1d_s=dq1d_s, q0g_s=q0g_s)
    for t in taps:
        if t.startswith("SBUF"):
            w = 2048 if t == "SBUFP" else (12288 if t == "SBUF3" else 16384)
            tap_outs[t] = nc.dram_tensor(
                "tap_" + t, [128, w], F32, kind="ExternalOutput").ap()
            continue
        src = dram_by_name[t]
        tap_outs[t] = nc.dram_tensor(
            "tap_" + t, list(src.shape), src.dtype, kind="ExternalOutput").ap()

    es = ExitStack()
    ctx = Ctx(nc)
    ctx._es = es
    with es:
        S1 = es.enter_context(nc.sbuf_tensor("S1", [128, 16384], F32))
        S2 = es.enter_context(nc.sbuf_tensor("S2", [128, 16384], F32))
        S3 = es.enter_context(nc.sbuf_tensor("S3", [128, 12288], F32))
        SP = es.enter_context(nc.sbuf_tensor("SP", [128, 2048], F32))
        PS = [es.enter_context(nc.psum_tensor("PS%d" % i, [128, 512], F32))
              for i in range(8)]
        blk = es.enter_context(nc.Block())

        def s1(a, b):
            return S1[:, a:b]

        def s2(a, b):
            return S2[:, a:b]

        def s3(a, b):
            return S3[:, a:b]

        def sp(a, b):
            return SP[:, a:b]

        ident = SP[:, 0:128]
        ones2d = SP[:, 128:256]
        qpc = SP[:, 256:264]
        ropec = SP[:, 1024:1536].rearrange("p (t f) -> p t f", f=64)
        ropes = SP[:, 1536:2048].rearrange("p (t f) -> p t f", f=64)

        C = ctx

        if on("PREP"):
            emit_prep(C, nc, locals())
        if on("A0"):
            emit_phase_A(C, nc, locals(), 0)
        if on("B0"):
            emit_phase_B(C, nc, locals(), 0)
        if on("G0"):
            emit_gather(C, nc, locals(), 0)
        if on("C0"):
            emit_ffn0(C, nc, locals())
        if on("D0"):
            emit_attn(C, nc, locals(), 0)
        if on("E0"):
            emit_phase_E(C, nc, locals(), 0)
        if on("A1"):
            emit_phase_A(C, nc, locals(), 1)
        if on("B1"):
            emit_phase_B(C, nc, locals(), 1)
        if on("G1"):
            emit_gather(C, nc, locals(), 1)
        if on("C1"):
            emit_ffn1(C, nc, locals())
        if on("D1"):
            emit_attn(C, nc, locals(), 1)
        if on("E1"):
            emit_phase_E(C, nc, locals(), 1)

        C.barrier()
        for t, dst in tap_outs.items():
            if t.startswith("SBUF"):
                slab = {"SBUF1": S1, "SBUF2": S2, "SBUF3": S3, "SBUFP": SP}[t]
                C.dma("gpsimd", dst, slab[:, :], "st_tap")
                continue
            src = dram_by_name[t]
            flat_s = src.flatten()
            flat_d = dst.flatten()
            n = flat_s.shape[0]
            step = 1 << 20
            for off in range(0, n, step):
                sl = slice(off, min(off + step, n))
                C.dma("gpsimd", flat_d[sl], flat_s[sl], "st_tap")

        C.barrier(engs=("gpsimd",))

        @blk.sync
        def _(e):
            ctx.replay("sync", e)

        @blk.tensor
        def _(e):
            ctx.replay("tensor", e)

        @blk.vector
        def _(e):
            ctx.replay("vector", e)

        @blk.scalar
        def _(e):
            ctx.replay("scalar", e)

        @blk.gpsimd
        def _(e):
            ctx.replay("gpsimd", e)

    return nc, tap_outs



def emit_prep(C, nc, env):
    g = env
    ident, ones2d, qpc = g["ident"], g["ones2d"], g["qpc"]
    ropec, ropes = g["ropec"], g["ropes"]
    s1, s2, s3, sp = g["s1"], g["s2"], g["s3"], g["sp"]
    PS = g["PS"]

    for slab in (g["S1"], g["S2"], g["S3"], g["SP"]):
        sv = slab[:, :]
        C.op("gpsimd", lambda e, sv=sv: e.memset(sv, 0.0), outs=[sv],
             inc="g_prep")
    C.barrier()

    C.op("gpsimd", lambda e: e.memset(ident, 0.0), outs=[ident], inc="g_prep")
    C.wait("gpsimd", "g_prep", C.cnt["g_prep"])
    C.op("gpsimd", lambda e: e.affine_select(
        out=ident, in_=ident, compare_op=ALU.not_equal, fill=1.0,
        base=0, pattern=[[-1, 128]], channel_multiplier=1),
        outs=[ident], ins=[ident], inc="g_prep")
    C.op("gpsimd", lambda e: e.memset(ones2d, 1.0), outs=[ones2d], inc="g_prep")
    C.dma("sync", qpc, g["qpos_cols"], "ld_prep")

    invf_i = sp(272, 336)
    bmask = sp(336, 400)
    bflt = sp(400, 464)
    invf_bc = sp(464, 528)
    C.op("gpsimd", lambda e: e.iota(
        invf_i.bitcast(I32), pattern=[[1, 64]], base=0,
        channel_multiplier=0), outs=[invf_i], inc="g_prep")
    C.wait("vector", "g_prep", C.cnt["g_prep"])
    C.wait("vector", "ld_prep", C.cnt["ld_prep"])
    C.op("vector", lambda e: e.memset(invf_bc, 1.0), outs=[invf_bc])
    for k in range(6):
        ck = float(np.float32(10000.0 ** (-(2 ** k) / 64.0)))
        C.op("vector", lambda e, k=k: e.tensor_scalar(
            bmask.bitcast(I32), invf_i.bitcast(I32), 1 << k, None,
            ALU.bitwise_and), outs=[bmask], ins=[invf_i])
        C.op("vector", lambda e: e.tensor_copy(bflt, bmask.bitcast(I32)),
             outs=[bflt], ins=[bmask])
        C.op("vector", lambda e: e.tensor_scalar(bflt, bflt, 0.5, None, ALU.is_gt),
             outs=[bflt], ins=[bflt])
        C.op("vector", lambda e, ck=ck: e.tensor_scalar(
            bflt, bflt, ck - 1.0, 1.0, ALU.mult, ALU.add),
            outs=[bflt], ins=[bflt])
        if k < 5:
            C.op("vector", lambda e: e.tensor_tensor(invf_bc, invf_bc, bflt, ALU.mult),
                 outs=[invf_bc], ins=[invf_bc, bflt])
        else:
            C.op("vector", lambda e: e.tensor_tensor(invf_bc, invf_bc, bflt, ALU.mult),
                 outs=[invf_bc], ins=[invf_bc, bflt], inc="v_prep")

    frq = sp(528, 592)
    u = sp(592, 656)
    nf = sp(656, 720)
    red = sp(720, 784)
    ni = sp(784, 848)
    m01 = sp(848, 912)
    red2 = sp(912, 976)

    def cascade(dst, x_ap, k_ap):
        C.op("vector", lambda e: e.scalar_tensor_tensor(
            out=red2, in0=k_ap, scalar=-CW1, in1=x_ap, op0=ALU.mult, op1=ALU.add),
            outs=[red2], ins=[k_ap, x_ap])
        C.op("vector", lambda e: e.scalar_tensor_tensor(
            out=dst, in0=k_ap, scalar=-CW2, in1=red2, op0=ALU.mult, op1=ALU.add),
            outs=[dst], ins=[k_ap, red2])
        C.op("vector", lambda e: e.scalar_tensor_tensor(
            out=red2, in0=k_ap, scalar=-CW3, in1=dst, op0=ALU.mult, op1=ALU.add),
            outs=[red2], ins=[k_ap, dst])
        C.op("vector", lambda e: e.tensor_copy(dst, red2), outs=[dst], ins=[red2])
    PI_F32 = float(np.float32(np.pi))
    PI_CLAMP = float(np.float32(3.1415925))

    def range_reduce_and_sin(dst, add_pio2):
        if add_pio2:
            C.op("vector", lambda e: e.tensor_scalar(
                u, frq, INV_2PI, 0.25, ALU.mult, ALU.add), outs=[u], ins=[frq])
        else:
            C.op("vector", lambda e: e.tensor_scalar_mul(u, frq, INV_2PI),
                 outs=[u], ins=[frq])
        C.op("vector", lambda e: e.tensor_copy(ni.bitcast(I32), u),
             outs=[ni], ins=[u])
        C.op("vector", lambda e: e.tensor_copy(nf, ni.bitcast(I32)),
             outs=[nf], ins=[ni])
        cascade(red, frq, nf)
        if add_pio2:
            C.op("vector", lambda e: e.tensor_scalar(
                red, red, PIO2_HI, PIO2_LO, ALU.add, ALU.add),
                outs=[red], ins=[red])
        C.op("vector", lambda e: e.tensor_scalar(m01, red, PI_F32, None, ALU.is_gt),
             outs=[m01], ins=[red])
        cascade(red, red, m01)
        C.op("vector", lambda e: e.tensor_scalar(
            red, red, PI_CLAMP, -PI_CLAMP, ALU.min, ALU.max),
            outs=[red], ins=[red], inc="v_prep")
        C.wait("scalar", "v_prep", C.cnt["v_prep"])
        C.op("scalar", lambda e, dst=dst: e.activation(out=dst, in_=red, func=AF.Sin),
             outs=[dst], ins=[red], inc="s_prep")

    for t in range(NT):
        C.wait("vector", "s_prep", C.cnt.get("s_prep", 0))
        C.op("vector", lambda e, t=t: e.tensor_scalar(
            frq, invf_bc, qpc[:, t:t + 1], None, ALU.mult),
            outs=[frq], ins=[invf_bc, qpc])
        range_reduce_and_sin(ropes[:, t, :], add_pio2=False)
        C.wait("vector", "s_prep", C.cnt.get("s_prep", 0))
        range_reduce_and_sin(ropec[:, t, :], add_pio2=True)

    q8ctr = [0, 0]
    for mi, (src, dst_s, dst_g, rows, cols) in enumerate((
            (g["p0g"], g["q0g_s"], g["q0g_g"], 256, 6144),
            (g["p0u"], g["q0u_s"], g["q0u_g"], 256, 6144),
            (g["p0d"], g["q0d_s"], g["q0d_g"], 768, 2048))):
        nch = rows // 128
        for chm in range(nch):
            sl = (mi * 8 + chm) % 2
            buf = s1(0, 8192) if sl == 0 else s1(8192, 16384)
            buf = buf[:, 0:cols]
            q8 = (s2(0, 2048) if sl == 0 else s2(2048, 4096))
            q8 = q8.bitcast(FP8)[:, 0:cols]
            lsem = "ldp8_%d" % sl
            vsem = "vp8_%d" % sl
            ssem = "stp8_%d" % sl
            C.wait("sync", vsem, C.cnt.get(vsem, 0))
            C.dma("sync", buf, src[chm * 128:(chm + 1) * 128, :], lsem)
            C.wait("vector", lsem, C.cnt[lsem])
            C.wait("vector", ssem, C.cnt.get(ssem, 0))
            C.op("vector", lambda e, q8=q8, buf=buf: e.tensor_copy(q8, buf),
                 outs=[q8], ins=[buf], inc=vsem)
            C.wait("gpsimd", vsem, C.cnt[vsem])
            C.dma("gpsimd", dst_s[chm * 128:(chm + 1) * 128, :], q8, ssem)
        C.wait("gpsimd", "stp8_0", C.cnt.get("stp8_0", 0))
        C.wait("gpsimd", "stp8_1", C.cnt.get("stp8_1", 0))
        C.op("gpsimd", lambda e, dst_s=dst_s, dst_g=dst_g: e.collective_compute(
            "AllGather", ALU.bypass, replica_groups=[list(range(8))],
            ins=[dst_s.opt()], outs=[dst_g.opt()]), inc="cc")

    def mx_dequant_transpose(src, dst_s, dst_g, rows, cols, mslot):
        nch = rows // 128
        nblk = cols // 32
        for chm in range(nch):
            sl = (mslot * 8 + chm) % 2
            buf = (s1(0, 8192) if sl == 0 else s1(8192, 16384))[:, 0:cols]
            dq = (s2(0, 8192) if sl == 0 else s2(8192, 16384))[:, 0:cols]
            amx = s3(0, nblk)
            scl = s3(256, 256 + nblk)
            rcp = s3(512, 512 + nblk)
            rc2 = s3(768, 768 + nblk)
            sc2 = s3(1024, 1024 + nblk)
            q8t = s3(1280, 1280 + ceil_div(cols, 4)).bitcast(FP8)[:, 0:cols]
            lsem = "ldmx_%d" % sl
            vsem = "vmx_%d" % sl
            C.wait("sync", "tmx_%d" % sl, C.cnt.get("tmx_%d" % sl, 0))
            C.dma("sync", buf, src[chm * 128:(chm + 1) * 128, :], lsem)
            C.wait("vector", lsem, C.cnt[lsem])
            bufv = buf.rearrange("p (b k) -> p b k", k=32)
            C.op("vector", lambda e, amx=amx, bufv=bufv: e.reduce_max(
                amx, bufv, axis=AX.X, apply_absolute_value=True),
                outs=[amx], ins=[buf])
            C.op("vector", lambda e, scl=scl, amx=amx: e.tensor_scalar(
                scl, amx, 1.0 / 448.0, 1e-30, ALU.mult, ALU.max),
                outs=[scl], ins=[amx])
            C.op("vector", lambda e, rcp=rcp, scl=scl: e.reciprocal(rcp, scl),
                 outs=[rcp], ins=[scl])
            C.op("vector", lambda e, rc2=rc2, rcp=rcp: e.tensor_scalar_mul(rc2, rcp, 0.5),
                 outs=[rc2], ins=[rcp])
            C.op("vector", lambda e, sc2=sc2, scl=scl: e.tensor_scalar_mul(sc2, scl, 2.0),
                 outs=[sc2], ins=[scl])
            q8v = q8t.rearrange("p (b k) -> p b k", k=32)
            C.op("vector", lambda e, q8v=q8v, bufv=bufv, rc2=rc2, nblk=nblk: e.tensor_tensor(
                q8v, bufv, rc2[:, 0:nblk].unsqueeze(2).broadcast_to([128, nblk, 32]), ALU.mult),
                outs=[q8t], ins=[buf, rc2])
            dqv = dq.rearrange("p (b k) -> p b k", k=32)
            C.op("vector", lambda e, dqv=dqv, q8v=q8v, sc2=sc2, nblk=nblk: e.tensor_tensor(
                dqv, q8v, sc2[:, 0:nblk].unsqueeze(2).broadcast_to([128, nblk, 32]), ALU.mult),
                outs=[dq], ins=[q8t, sc2], inc=vsem)
            C.wait("tensor", vsem, C.cnt[vsem])
            ncc = cols // 128
            for cc in range(ncc):
                psl = cc % 2
                pban = PS[psl]
                C.wait("tensor", "smx_%d" % psl, C.cnt.get("smx_%d" % psl, 0))
                C.op("tensor", lambda e, pban=pban, dq=dq, cc=cc: e.transpose(
                    pban[:, 0:128], dq[:, cc * 128:(cc + 1) * 128], g["ident"]),
                    inc="tmx_%d" % sl)
                C.wait("scalar", "tmx_%d" % sl, C.cnt["tmx_%d" % sl])
                ev = s3(4096 + psl * 128, 4096 + psl * 128 + 128)
                C.wait("scalar", "stmx_%d" % psl, C.cnt.get("stmx_%d" % psl, 0))
                C.op("scalar", lambda e, ev=ev, pban=pban: e.copy(out=ev, in_=pban[:, 0:128]),
                     outs=[ev], inc="smx_%d" % psl)
                C.wait("gpsimd", "smx_%d" % psl, C.cnt["smx_%d" % psl])
                C.dma("gpsimd", dst_s[cc * 128:(cc + 1) * 128,
                                      chm * 128:(chm + 1) * 128], ev, "stmx_%d" % psl)
        C.wait("gpsimd", "stmx_0", C.cnt.get("stmx_0", 0))
        C.wait("gpsimd", "stmx_1", C.cnt.get("stmx_1", 0))
        C.op("gpsimd", lambda e, dst_s=dst_s, dst_g=dst_g: e.collective_compute(
            "AllGather", ALU.bypass, replica_groups=[list(range(8))],
            ins=[dst_s.opt()], outs=[dst_g.opt()]), inc="cc")

    C.barrier()
    mx_dequant_transpose(g["p1g"], g["dq1g_s"], g["dq1g_g"], 1024, 2048, 0)
    mx_dequant_transpose(g["p1u"], g["dq1u_s"], g["dq1u_g"], 1024, 2048, 1)
    mx_dequant_transpose(g["p1d"], g["dq1d_s"], g["dq1d_g"], 256, 8192, 0)

    C.barrier()


def emit_phase_A(C, nc, env, layer):
    g = env
    s1, s2, s3, sp = g["s1"], g["s2"], g["s3"], g["sp"]
    PS = g["PS"]
    ident = g["ident"]
    x_src = g["x_in"] if layer == 0 else g["x0d"]
    xn_fm = s1(0, 16384).rearrange("p (c t) -> p c t", t=1024)
    dqx_fm = s2(0, 16384).rearrange("p (c t) -> p c t", t=1024) if layer == 1 else None

    pre = "a"
    xt_done = []
    tile_done = []
    for t in range(NT):
        sl = t % 2
        xt = s3(0, 2048) if sl == 0 else s3(2048, 4096)
        xn_tm = s3(4096, 6144) if sl == 0 else s3(6144, 8192)
        bns = s3(8192, 8216)
        mv = s3(8216, 8218)
        uvar = s3(8218, 8219)
        sd = s3(8219, 8220)
        rstd = s3(8220, 8221)
        nbias = s3(8221, 8222)
        lsem = "lda_%d" % sl
        if t >= 2:
            C.wait("sync", "s_" + pre, xt_done[t - 2])
        C.dma("sync", xt, x_src[t * 128:(t + 1) * 128, :], lsem)
        C.wait("vector", lsem, C.cnt[lsem])
        for j in range(4):
            C.op("vector", lambda e, bns=bns, xt=xt, j=j: e.bn_stats(
                bns[:, j * 6:(j + 1) * 6], xt[:, j * 512:(j + 1) * 512]),
                outs=[bns], ins=[xt])
        C.op("vector", lambda e, mv=mv, bns=bns: e.bn_aggr(mv, bns),
             outs=[mv], ins=[bns])
        C.op("vector", lambda e, uvar=uvar, mv=mv: e.tensor_scalar_add(
            uvar, mv[:, 1:2], LN_EPS), outs=[uvar], ins=[mv], inc="v_" + pre)
        C.wait("scalar", "v_" + pre, C.cnt["v_" + pre])
        C.op("scalar", lambda e, sd=sd, uvar=uvar: e.sqrt(out=sd, in_=uvar),
             outs=[sd], ins=[uvar], inc="s_" + pre)
        C.wait("vector", "s_" + pre, C.cnt["s_" + pre])
        C.op("vector", lambda e, rstd=rstd, sd=sd: e.reciprocal(rstd, sd),
             outs=[rstd], ins=[sd])
        C.op("vector", lambda e, nbias=nbias, mv=mv, rstd=rstd: e.scalar_tensor_tensor(
            out=nbias, in0=mv[:, 0:1], scalar=-1.0, in1=rstd,
            op0=ALU.mult, op1=ALU.mult), outs=[nbias], ins=[mv, rstd], inc="v_" + pre)
        C.wait("scalar", "v_" + pre, C.cnt["v_" + pre])
        if t >= 2:
            C.wait("scalar", "t_" + pre, tile_done[t - 2][0])
            if layer == 1:
                C.wait("scalar", "gxn%d_" % sl + pre, tile_done[t - 2][2])
        C.op("scalar", lambda e, xn_tm=xn_tm, xt=xt, rstd=rstd, nbias=nbias: e.activation(
            out=xn_tm, in_=xt, func=AF.Identity, scale=rstd, bias=nbias),
            outs=[xn_tm], ins=[xt], inc="s_" + pre)
        xt_done.append(C.cnt["s_" + pre])
        C.wait("tensor", "s_" + pre, C.cnt["s_" + pre])
        for cc in range(16):
            pban = PS[cc % 2]
            C.wait("tensor", "ps%d_" % (cc % 2) + pre,
                   C.cnt.get("ps%d_" % (cc % 2) + pre, 0))
            C.op("tensor", lambda e, pban=pban, xn_tm=xn_tm, cc=cc: e.transpose(
                pban[:, 0:128], xn_tm[:, cc * 128:(cc + 1) * 128], ident),
                inc="t_" + pre)
            eng = "scalar" if cc % 2 == 0 else "vector"
            C.wait(eng, "t_" + pre, C.cnt["t_" + pre])
            if eng == "scalar":
                C.op(eng, lambda e, pban=pban, cc=cc, t=t: e.copy(
                    out=xn_fm[:, cc, t * 128:(t + 1) * 128], in_=pban[:, 0:128]),
                    outs=[xn_fm], inc="ps%d_" % (cc % 2) + pre)
            else:
                C.op(eng, lambda e, pban=pban, cc=cc, t=t: e.tensor_copy(
                    xn_fm[:, cc, t * 128:(t + 1) * 128], pban[:, 0:128]),
                    outs=[xn_fm], inc="ps%d_" % (cc % 2) + pre)
        if layer == 1:
            C.wait("gpsimd", "s_" + pre, C.cnt["s_" + pre])
            C.dma("gpsimd", g["xnd"][t * 128:(t + 1) * 128, :], xn_tm,
                  "gxn%d_" % sl + pre)
        tile_done.append((C.cnt["t_" + pre], C.cnt.get("v_" + pre, 0),
                          C.cnt.get("gxn%d_" % sl + pre, 0)))
    C.barrier()


def emit_phase_B(C, nc, env, layer):
    g = env
    s1, s2, s3, sp = g["s1"], g["s2"], g["s3"], g["sp"]
    PS = g["PS"]
    ident = g["ident"]
    ones2d = g["ones2d"]
    ropec, ropes = g["ropec"], g["ropes"]
    xn_fm = s1(0, 16384).rearrange("p (c t) -> p c t", t=1024)
    pre = "b"

    if layer == 0:
        plan = [("q", g["w0q_t"], g["b0q"], 0, 4), ("k", g["w0k_t"], g["b0k"], 0, 1),
                ("v", g["w0v_t"], g["b0v"], 0, 1)]
        wsteps = 2
    else:
        plan = [("q", g["w1q_t"], g["b1q"], 0, 4), ("k", g["w1k_t"], g["b1k"], 0, 4),
                ("v", g["w1v_t"], g["b1v"], 0, 4)]

    brow = {}
    off = 3200
    for kind, wt, bi, _, nch in plan:
        width = nch * 512
        brow[kind] = s3(off, off + width)[0:1, :]
        C.dma("sync", brow[kind], bi[:, :], "ldpro_" + pre)
        off += width

    wstage = [s2(0, 8192).rearrange("p (c o) -> p c o", o=512),
              s2(8192, 16384).rearrange("p (c o) -> p c o", o=512)]
    ochunks = []
    for kind, wt, bi, _, nch in plan:
        for oc in range(nch):
            ochunks.append((kind, wt, oc))

    gaincol = s3(2688, 2704)
    if layer == 0:
        C.dma("sync", gaincol, g["gain0"], "ldpro_" + pre)
        C.wait("vector", "ldpro_" + pre, C.cnt["ldpro_" + pre])
        C.op("vector", lambda e: e.tensor_scalar_mul(gaincol, gaincol, SCALE_HD),
             outs=[gaincol], ins=[gaincol], inc="v_" + pre)

    def load_w(i):
        kind, wt, oc = ochunks[i]
        dst = wstage[i % 2]
        src = wt.rearrange("(c p) o -> p c o", p=128)[:, :, oc * 512:(oc + 1) * 512]
        C.dma("sync", dst, src, "lw%d_" % (i % 2) + pre)

    nO = len(ochunks)
    mm_done = []
    st = dict(ps_done={0: 0, 1: 0})
    env["_Bst"] = st
    for i in range(min(2, nO)):
        load_w(i)

    C.wait("tensor", "ldpro_" + pre, C.cnt["ldpro_" + pre])
    for i, (kind, wt, oc) in enumerate(ochunks):
        wbuf = wstage[i % 2]
        C.wait("tensor", "lw%d_" % (i % 2) + pre, C.cnt["lw%d_" % (i % 2) + pre])
        for t in range(NT):
            bank = t % 2
            ps = PS[bank]
            C.wait("tensor", "s_" + pre, st["ps_done"][bank])
            for dc in range(16):
                C.op("tensor", lambda e, ps=ps, dc=dc, t=t, wbuf=wbuf: e.matmul(
                    ps[:, :], xn_fm[:, dc, t * 128:(t + 1) * 128], wbuf[:, dc, :],
                    start=(dc == 0), stop=False))
            C.op("tensor", lambda e, ps=ps, kind=kind, oc=oc: e.matmul(
                ps[:, :], ones2d[0:1, :], brow[kind][:, oc * 512:(oc + 1) * 512],
                start=False, stop=True), inc="t_" + pre)
            emit_B_evac(C, nc, env, layer, kind, oc, t, ps, pre, gaincol)
        mm_done.append(C.cnt["t_" + pre])
        if i + 2 < nO:
            C.wait("sync", "t_" + pre, mm_done[i])
            load_w(i + 2)
    C.barrier()


def emit_B_evac(C, nc, env, layer, kind, oc, t, ps, pre, gaincol):
    g = env
    s3 = g["s3"]
    PS = g["PS"]
    ident = g["ident"]
    ropec, ropes = g["ropec"], g["ropes"]
    st = env["_Bst"]
    st.setdefault("qkv_done", {0: None, 1: None})
    st.setdefault("qr_done", {0: 0, 1: 0})
    st.setdefault("tb_done", {0: 0, 1: 0})
    st.setdefault("ev_done", {0: 0, 1: 0})
    sl = t % 2
    bank = sl
    qkv = s3(0, 512) if sl == 0 else s3(512, 1024)
    qr = s3(1024, 1536) if sl == 0 else s3(1536, 2048)
    sq = s3(2048, 2560)
    msq4 = s3(2560, 2564)
    uu4 = s3(2564, 2568)
    sd4 = s3(2568, 2572)
    rs4 = s3(2572, 2576)
    evst = [s3(2832, 2960), s3(2960, 3088)]
    tmp1 = sq[:, 256:512].rearrange("p (h k) -> p h k", k=64)
    tmp2 = sq[:, 0:256].rearrange("p (h k) -> p h k", k=64)
    cosT = ropec[:, t, :].unsqueeze(1).broadcast_to([128, 4, 64])
    sinT = ropes[:, t, :].unsqueeze(1).broadcast_to([128, 4, 64])

    C.wait("scalar", "t_" + pre, C.cnt["t_" + pre])
    need_rope = (kind in ("q", "k"))
    if st["qkv_done"][sl] is not None:
        sem, val = st["qkv_done"][sl]
        C.wait("scalar", sem, val)
    C.op("scalar", lambda e, qkv=qkv, ps=ps: e.copy(out=qkv, in_=ps[:, :]),
         outs=[qkv], inc="s_" + pre)
    st["ps_done"][bank] = C.cnt["s_" + pre]

    if not need_rope:
        C.wait("gpsimd", "s_" + pre, C.cnt["s_" + pre])
        vdst = g["v0s"] if layer == 0 else g["v1s"]
        C.dma("gpsimd", vdst[t * 128:(t + 1) * 128, oc * 512:(oc + 1) * 512],
              qkv, "gv%d_" % sl + pre)
        st["qkv_done"][sl] = ("gv%d_" % sl + pre, C.cnt["gv%d_" % sl + pre])
        return

    qv = qkv.rearrange("p (h d) -> p h d", d=128)
    qrv = qr.rearrange("p (h d) -> p h d", d=128)
    C.wait("scalar", "v_" + pre, st.get("sq_free", 0))
    if layer == 0:
        C.op("scalar", lambda e: e.activation(out=sq, in_=qkv, func=AF.Square),
             outs=[sq], ins=[qkv], inc="s_" + pre)
        C.wait("vector", "s_" + pre, C.cnt["s_" + pre])
        C.wait("vector", "t2_" + pre, st["qr_done"][sl])
        C.op("vector", lambda e: e.tensor_reduce(
            msq4, sq.rearrange("p (h d) -> p h d", d=128), axis=AX.X,
            op=ALU.add), outs=[msq4], ins=[sq])
        C.op("vector", lambda e: e.tensor_scalar(
            uu4, msq4, 1.0 / 128.0, RMS_EPS, ALU.mult, ALU.add),
            outs=[uu4], ins=[msq4], inc="v_" + pre)
        C.wait("scalar", "v_" + pre, C.cnt["v_" + pre])
        C.op("scalar", lambda e: e.sqrt(out=sd4, in_=uu4),
             outs=[sd4], ins=[uu4], inc="s_" + pre)
        C.wait("vector", "s_" + pre, C.cnt["s_" + pre])
        C.op("vector", lambda e: e.reciprocal(rs4, sd4),
             outs=[rs4], ins=[sd4])
        if kind == "q":
            C.op("vector", lambda e, oc=oc: e.tensor_tensor(
                rs4, rs4, gaincol[:, oc * 4:(oc + 1) * 4], ALU.mult),
                outs=[rs4], ins=[rs4, gaincol])
        C.op("vector", lambda e: e.tensor_tensor(
            qv, qv, rs4.unsqueeze(2).broadcast_to([128, 4, 128]), ALU.mult),
            outs=[qkv], ins=[qkv, rs4])
        q1 = qv[:, :, 0:64]; q2 = qv[:, :, 64:128]
        r1 = qrv[:, :, 0:64]; r2 = qrv[:, :, 64:128]
        C.op("vector", lambda e: e.tensor_tensor(tmp1, q1, cosT, ALU.mult),
             outs=[tmp1], ins=[qkv])
        C.op("vector", lambda e: e.tensor_tensor(tmp2, q2, sinT, ALU.mult),
             outs=[sq], ins=[qkv])
        C.op("vector", lambda e: e.tensor_tensor(r1, tmp1, tmp2, ALU.add),
             outs=[qr], ins=[tmp1, sq])
        C.op("vector", lambda e: e.tensor_tensor(tmp1, q1, sinT, ALU.mult),
             outs=[tmp1], ins=[qkv])
        C.op("vector", lambda e: e.tensor_tensor(tmp2, q2, cosT, ALU.mult),
             outs=[sq], ins=[qkv])
        C.op("vector", lambda e: e.tensor_tensor(r2, tmp2, tmp1, ALU.subtract),
             outs=[qr], ins=[tmp1, sq], inc="v_" + pre)
    else:
        C.wait("vector", "s_" + pre, C.cnt["s_" + pre])
        C.wait("vector", "t2_" + pre, st["qr_done"][sl])
        qp = qkv.rearrange("p (h d two) -> p h d two", d=64, two=2)
        rp = qr.rearrange("p (h d two) -> p h d two", d=64, two=2)
        qe = qp[:, :, :, 0]; qo = qp[:, :, :, 1]
        re_ = rp[:, :, :, 0]; ro = rp[:, :, :, 1]
        C.op("vector", lambda e: e.tensor_tensor(tmp1, qe, cosT, ALU.mult),
             outs=[tmp1], ins=[qkv])
        C.op("vector", lambda e: e.tensor_tensor(tmp2, qo, sinT, ALU.mult),
             outs=[sq], ins=[qkv])
        C.op("vector", lambda e: e.tensor_tensor(re_, tmp1, tmp2, ALU.subtract),
             outs=[qr], ins=[tmp1, sq])
        C.op("vector", lambda e: e.tensor_tensor(tmp1, qe, sinT, ALU.mult),
             outs=[tmp1], ins=[qkv])
        C.op("vector", lambda e: e.tensor_tensor(tmp2, qo, cosT, ALU.mult),
             outs=[sq], ins=[qkv])
        if kind == "q":
            C.op("vector", lambda e: e.tensor_tensor(ro, tmp1, tmp2, ALU.add),
                 outs=[qr], ins=[tmp1, sq])
            C.op("vector", lambda e: e.tensor_scalar_mul(qr, qr, SCALE_HD),
                 outs=[qr], ins=[qr], inc="v_" + pre)
        else:
            C.op("vector", lambda e: e.tensor_tensor(ro, tmp1, tmp2, ALU.add),
                 outs=[qr], ins=[tmp1, sq], inc="v_" + pre)
    st["qkv_done"][sl] = ("v_" + pre, C.cnt["v_" + pre])
    st["sq_free"] = C.cnt["v_" + pre]

    C.wait("tensor", "v_" + pre, C.cnt["v_" + pre])
    for hs in range(4):
        h = oc * 4 + hs
        tb = hs % 2
        pban = PS[2 + tb]
        evsem = "sv2_%d_" % tb + pre
        C.wait("tensor", evsem, st["tb_done"][tb])
        C.op("tensor", lambda e, pban=pban, hs=hs: e.transpose(
            pban[:, 0:128], qr[:, hs * 128:(hs + 1) * 128], ident),
            inc="t2_" + pre)
        eng = "scalar" if tb == 0 else "vector"
        C.wait(eng, "t2_" + pre, C.cnt["t2_" + pre])
        C.wait(eng, "gst%d_" % tb + pre, st["ev_done"][tb])
        ev = evst[tb]
        if eng == "scalar":
            C.op(eng, lambda e, ev=ev, pban=pban: e.copy(out=ev, in_=pban[:, 0:128]),
                 outs=[ev], inc=evsem)
        else:
            C.op(eng, lambda e, ev=ev, pban=pban: e.tensor_copy(ev, pban[:, 0:128]),
                 outs=[ev], inc=evsem)
        st["tb_done"][tb] = C.cnt[evsem]
        C.wait("gpsimd", evsem, C.cnt[evsem])
        if kind == "q":
            C.dma("gpsimd", g["qd"][h, :, t * 128:(t + 1) * 128], ev,
                  "gst%d_" % tb + pre)
        else:
            kdst = g["k0s"] if layer == 0 else g["k1s"]
            C.dma("gpsimd", kdst[h * 128:(h + 1) * 128, t * 128:(t + 1) * 128], ev,
                  "gst%d_" % tb + pre)
        st["ev_done"][tb] = C.cnt["gst%d_" % tb + pre]
    st["qr_done"][sl] = C.cnt["t2_" + pre]


def emit_gather(C, nc, env, layer):
    g = env
    C.barrier(engs=("gpsimd",))
    ks, kg = (g["k0s"], g["k0g"]) if layer == 0 else (g["k1s"], g["k1g"])
    vs, vg = (g["v0s"], g["v0g"]) if layer == 0 else (g["v1s"], g["v1g"])
    C.op("gpsimd", lambda e: e.collective_compute(
        "AllGather", ALU.bypass, replica_groups=RG,
        ins=[ks.opt()], outs=[kg.opt()]), inc="cc")
    C.op("gpsimd", lambda e: e.collective_compute(
        "AllGather", ALU.bypass, replica_groups=RG,
        ins=[vs.opt()], outs=[vg.opt()]), inc="cc")


def emit_ffn0(C, nc, env):
    g = env
    s1, s2, s3, sp = g["s1"], g["s2"], g["s3"], g["sp"]
    PS = g["PS"]
    pre = "c"
    xn_fm = s1(0, 16384).rearrange("p (c t) -> p c t", t=1024)
    hi = s2(0, 8192).bitcast(BF16).rearrange("p (c t) -> p c t", t=1024)
    lo = s2(8192, 16384).bitcast(BF16).rearrange("p (c t) -> p c t", t=1024)
    for cc in range(16):
        C.op("scalar", lambda e, cc=cc: e.copy(out=hi[:, cc, :], in_=xn_fm[:, cc, :]),
             outs=[hi], ins=[xn_fm], inc="s_" + pre)
        C.wait("vector", "s_" + pre, C.cnt["s_" + pre])
        C.op("vector", lambda e, cc=cc: e.tensor_tensor(
            lo[:, cc, :], xn_fm[:, cc, :], hi[:, cc, :], ALU.subtract),
            outs=[lo], ins=[xn_fm, hi], inc="v_" + pre)
    C.barrier()

    fh = s1(0, 6144).bitcast(BF16).rearrange("p (c t) -> p c t", t=256)
    fl = s1(6144, 12288).bitcast(BF16).rearrange("p (c t) -> p c t", t=256)
    gw = [s1(12288, 12800).bitcast(FP8).rearrange("p (c i) -> p c i", i=128),
          s1(12800, 13312).bitcast(FP8).rearrange("p (c i) -> p c i", i=128)]
    uw = [s1(13312, 13824).bitcast(FP8).rearrange("p (c i) -> p c i", i=128),
          s1(13824, 14336).bitcast(FP8).rearrange("p (c i) -> p c i", i=128)]
    gsc = s1(14336, 14384); usc = s1(14384, 14432)
    gate_t = s1(14432, 14944)[:, 0:256]
    fuse32 = s1(14944, 15456)[:, 0:256]
    dwst = [s3(0, 3072).bitcast(FP8).rearrange("p (c e) -> p c e", e=256),
            s3(3072, 6144).bitcast(FP8).rearrange("p (c e) -> p c e", e=256)]
    dsbc = s3(6144, 8192)
    gout = [s3(8192, 8704), s3(8704, 9216)]
    gtmp = [s3(9216, 9728), s3(9728, 10240)]

    C.dma("sync", gsc, g["gs0"], "ldpro_" + pre)
    C.dma("sync", usc, g["us0"], "ldpro_" + pre)
    C.dma("sync", dsbc, g["ds0bc"], "ldpro_" + pre)

    wq8g, wq8u, wq8d = g["q0g_g"], g["q0u_g"], g["q0d_g"]
    lw_done = {}
    lw2_done = {}
    fuse_done = {}
    down_read = 0
    ps_done = {0: 0, 1: 0}
    psd_done = {2: 0, 3: 0}
    gout_done = {0: 0, 1: 0}
    gtmp_done = {0: 0, 1: 0}

    def load_gu(tb, ic, buf):
        src_g = wq8g.rearrange("(c p) i -> p c i", p=128)[:, :, ic * 128:(ic + 1) * 128]
        src_u = wq8u.rearrange("(c p) i -> p c i", p=128)[:, :, ic * 128:(ic + 1) * 128]
        C.wait("sync", "t_" + pre, lw_done.get((tb, ic - 2), 0))
        C.dma("sync", gw[buf], src_g, "lwg%d_" % buf + pre)
        C.dma("sync", uw[buf], src_u, "lwg%d_" % buf + pre)

    def load_dw(tb, ec, buf):
        src = wq8d.rearrange("(c p) e -> p c e", p=128)[:, :, ec * 256:(ec + 1) * 256]
        C.wait("sync", "t2_" + pre, lw2_done.get((tb, ec - 2), 0))
        C.dma("sync", dwst[buf], src, "lwd%d_" % buf + pre)

    C.wait("scalar", "ldpro_" + pre, C.cnt["ldpro_" + pre])
    C.wait("vector", "ldpro_" + pre, C.cnt["ldpro_" + pre])
    for tb in range(4):
        tsl = slice(tb * 256, (tb + 1) * 256)
        load_gu(tb, 0, 0)
        load_gu(tb, 1, 1)
        for ic in range(48):
            buf = ic % 2
            C.wait("tensor", "lwg%d_" % buf + pre, C.cnt["lwg%d_" % buf + pre])
            psg, psu = PS[0], PS[1]
            C.wait("tensor", "s_" + pre, ps_done[0])
            C.wait("tensor", "v_" + pre, ps_done[1])
            if tb > 0 and ic == 0:
                C.wait("tensor", "t2_" + pre, down_read)
            for hc in range(16):
                C.op("tensor", lambda e, hc=hc, buf=buf, psg=psg, tsl=tsl: e.matmul(
                    psg[:, 0:256], gw[buf][:, hc, :], hi[:, hc, tsl],
                    start=(hc == 0), stop=False))
                C.op("tensor", lambda e, hc=hc, buf=buf, psg=psg, tsl=tsl: e.matmul(
                    psg[:, 0:256], gw[buf][:, hc, :], lo[:, hc, tsl],
                    start=False, stop=(hc == 15)))
            for hc in range(16):
                C.op("tensor", lambda e, hc=hc, buf=buf, psu=psu, tsl=tsl: e.matmul(
                    psu[:, 0:256], uw[buf][:, hc, :], hi[:, hc, tsl],
                    start=(hc == 0), stop=False))
                C.op("tensor", lambda e, hc=hc, buf=buf, psu=psu, tsl=tsl: e.matmul(
                    psu[:, 0:256], uw[buf][:, hc, :], lo[:, hc, tsl],
                    start=False, stop=(hc == 15)),
                    inc=("t_" + pre) if hc == 15 else None)
            lw_done[(tb, ic)] = C.cnt["t_" + pre]
            if ic + 2 < 48:
                load_gu(tb, ic + 2, buf)
            C.wait("scalar", "t_" + pre, C.cnt["t_" + pre])
            C.op("scalar", lambda e, psg=psg, ic=ic: e.activation(
                out=gate_t, in_=psg[:, 0:256], func=AF.Silu, scale=gsc[:, ic:ic + 1]),
                outs=[gate_t], ins=[gsc], inc="s_" + pre)
            ps_done[0] = C.cnt["s_" + pre]
            C.wait("vector", "s_" + pre, C.cnt["s_" + pre])
            C.op("vector", lambda e, psu=psu, ic=ic: e.scalar_tensor_tensor(
                out=fuse32, in0=psu[:, 0:256], scalar=usc[:, ic:ic + 1], in1=gate_t,
                op0=ALU.mult, op1=ALU.mult), outs=[fuse32], ins=[usc, gate_t],
                inc="v_" + pre)
            ps_done[1] = C.cnt["v_" + pre]
            C.wait("scalar", "v_" + pre, C.cnt["v_" + pre])
            C.op("scalar", lambda e, ic=ic: e.copy(out=fh[:, ic, :], in_=fuse32),
                 outs=[fh], ins=[fuse32], inc="s_" + pre)
            C.wait("vector", "s_" + pre, C.cnt["s_" + pre])
            C.op("vector", lambda e, ic=ic: e.tensor_tensor(
                fl[:, ic, :], fuse32, fh[:, ic, :], ALU.subtract),
                outs=[fl], ins=[fuse32, fh], inc="v_" + pre)
        fuse_done[tb] = C.cnt["v_" + pre]

        load_dw(tb, 0, 0)
        load_dw(tb, 1, 1)
        C.wait("tensor", "v_" + pre, fuse_done[tb])
        for ec in range(8):
            buf = ec % 2
            C.wait("tensor", "lwd%d_" % buf + pre, C.cnt["lwd%d_" % buf + pre])
            for ts_ in range(2):
                t_abs = tb * 2 + ts_
                bank = 2 + ts_
                psd = PS[bank]
                C.wait("tensor", "v2_" + pre, psd_done[bank])
                for ic in range(48):
                    C.op("tensor", lambda e, ic=ic, ts_=ts_, psd=psd, buf=buf: e.matmul(
                        psd[:, 0:256], fh[:, ic, ts_ * 128:(ts_ + 1) * 128],
                        dwst[buf][:, ic, :], start=(ic == 0), stop=False))
                    C.op("tensor", lambda e, ic=ic, ts_=ts_, psd=psd, buf=buf: e.matmul(
                        psd[:, 0:256], fl[:, ic, ts_ * 128:(ts_ + 1) * 128],
                        dwst[buf][:, ic, :], start=False, stop=(ic == 47)),
                        inc=("t2_" + pre) if ic == 47 else None)
                C.wait("vector", "t2_" + pre, C.cnt["t2_" + pre])
                C.wait("vector", "s2_" + pre, gtmp_done[ts_])
                C.op("vector", lambda e, psd=psd, ec=ec, ts_=ts_: e.tensor_tensor(
                    gtmp[ts_][:, 0:256], psd[:, 0:256],
                    dsbc[:, ec * 256:(ec + 1) * 256], ALU.mult),
                    outs=[gtmp[ts_]], ins=[dsbc], inc="v2_" + pre)
                psd_done[bank] = C.cnt["v2_" + pre]
                C.wait("scalar", "v2_" + pre, C.cnt["v2_" + pre])
                C.wait("scalar", "gst%d_" % ts_ + pre, gout_done[ts_])
                C.op("scalar", lambda e, ts_=ts_: e.activation(
                    out=gout[ts_][:, 0:256], in_=gtmp[ts_][:, 0:256], func=AF.Silu),
                    outs=[gout[ts_]], ins=[gtmp[ts_]], inc="s2_" + pre)
                gtmp_done[ts_] = C.cnt["s2_" + pre]
                C.wait("gpsimd", "s2_" + pre, C.cnt["s2_" + pre])
                C.dma("gpsimd", g["gd"][t_abs, :, ec * 256:(ec + 1) * 256],
                      gout[ts_][:, 0:256], "gst%d_" % ts_ + pre)
                gout_done[ts_] = C.cnt["gst%d_" % ts_ + pre]
            lw2_done[(tb, ec)] = C.cnt["t2_" + pre]
            if ec + 2 < 8:
                load_dw(tb, ec + 2, buf)
        down_read = C.cnt["t2_" + pre]
    C.barrier()


def emit_ffn1(C, nc, env):
    g = env
    s1, s2, s3, sp = g["s1"], g["s2"], g["s3"], g["sp"]
    PS = g["PS"]
    ident = g["ident"]
    pre = "c"
    ident = g["ident"]
    dqx_fm = s1(0, 16384).rearrange("p (c t) -> p c t", t=1024)
    PS = g["PS"]
    xq_done = {0: 0, 1: 0}
    pb_done = {0: 0, 1: 0}
    tq_done = {}
    for t in range(NT):
        sl = t % 2
        xnt = s3(0, 2048) if sl == 0 else s3(2048, 4096)
        dqt = s3(4096, 6144) if sl == 0 else s3(6144, 8192)
        amx0 = s3(8192, 8256); scl0 = s3(8256, 8320); rcp0 = s3(8320, 8384)
        rc20 = s3(8384, 8448); sc20 = s3(8448, 8512)
        q8t0 = s3(8512, 9024).bitcast(FP8)[:, 0:2048]
        C.wait("sync", "vq%d_" % sl + pre, xq_done[sl])
        C.dma("sync", xnt, g["xnd"][t * 128:(t + 1) * 128, :], "lxq%d_" % sl + pre)
        C.wait("vector", "lxq%d_" % sl + pre, C.cnt["lxq%d_" % sl + pre])
        xnv = xnt.rearrange("p (b k) -> p b k", k=32)
        C.op("vector", lambda e, amx0=amx0, xnv=xnv: e.reduce_max(
            amx0, xnv, axis=AX.X, apply_absolute_value=True),
            outs=[amx0], ins=[xnt])
        C.op("vector", lambda e, scl0=scl0, amx0=amx0: e.tensor_scalar(
            scl0, amx0, 1.0 / 448.0, 1e-30, ALU.mult, ALU.max),
            outs=[scl0], ins=[amx0])
        C.op("vector", lambda e, rcp0=rcp0, scl0=scl0: e.reciprocal(rcp0, scl0),
             outs=[rcp0], ins=[scl0])
        C.op("vector", lambda e, rc20=rc20, rcp0=rcp0: e.tensor_scalar_mul(rc20, rcp0, 0.5),
             outs=[rc20], ins=[rcp0])
        C.op("vector", lambda e, sc20=sc20, scl0=scl0: e.tensor_scalar_mul(sc20, scl0, 2.0),
             outs=[sc20], ins=[scl0])
        q8v0 = q8t0.rearrange("p (b k) -> p b k", k=32)
        C.op("vector", lambda e, q8v0=q8v0, xnv=xnv, rc20=rc20: e.tensor_tensor(
            q8v0, xnv, rc20.unsqueeze(2).broadcast_to([128, 64, 32]), ALU.mult),
            outs=[q8t0], ins=[xnt, rc20])
        dqv0 = dqt.rearrange("p (b k) -> p b k", k=32)
        C.wait("vector", "tq_" + pre, tq_done.get(sl, 0))
        C.op("vector", lambda e, dqv0=dqv0, q8v0=q8v0, sc20=sc20: e.tensor_tensor(
            dqv0, q8v0, sc20.unsqueeze(2).broadcast_to([128, 64, 32]), ALU.mult),
            outs=[dqt], ins=[q8t0, sc20], inc="vq%d_" % sl + pre)
        xq_done[sl] = C.cnt["vq%d_" % sl + pre]
        C.wait("tensor", "vq%d_" % sl + pre, C.cnt["vq%d_" % sl + pre])
        for cc in range(16):
            bb = cc % 2
            pban = PS[bb]
            C.wait("tensor", "pq%d_" % bb + pre, pb_done[bb])
            C.op("tensor", lambda e, pban=pban, dqt=dqt, cc=cc: e.transpose(
                pban[:, 0:128], dqt[:, cc * 128:(cc + 1) * 128], ident),
                inc="tq_" + pre)
            eng = "scalar" if bb == 0 else "vector"
            C.wait(eng, "tq_" + pre, C.cnt["tq_" + pre])
            if eng == "scalar":
                C.op(eng, lambda e, pban=pban, cc=cc, t=t: e.copy(
                    out=dqx_fm[:, cc, t * 128:(t + 1) * 128], in_=pban[:, 0:128]),
                    outs=[dqx_fm], inc="pq%d_" % bb + pre)
            else:
                C.op(eng, lambda e, pban=pban, cc=cc, t=t: e.tensor_copy(
                    dqx_fm[:, cc, t * 128:(t + 1) * 128], pban[:, 0:128]),
                    outs=[dqx_fm], inc="pq%d_" % bb + pre)
            pb_done[bb] = C.cnt["pq%d_" % bb + pre]
        tq_done[sl] = C.cnt["tq_" + pre]
    C.barrier()

    gw = [s2(0, 4096).rearrange("p (c o) -> p c o", o=256),
          s2(4096, 8192).rearrange("p (c o) -> p c o", o=256)]
    uw = [s2(8192, 12288).rearrange("p (c o) -> p c o", o=256),
          s2(12288, 16384).rearrange("p (c o) -> p c o", o=256)]
    dw = [s3(0, 4096).rearrange("p (c e) -> p c e", e=2048),
          s3(4096, 8192).rearrange("p (c e) -> p c e", e=2048)]
    dpart = s3(8192, 12288).rearrange("p (u e) -> p u e", e=2048)
    fuse_t = sp(1024, 1280)
    gate_t = sp(1280, 1536)
    dqf_t = sp(1536, 1792)
    amx = sp(264, 272); scl = sp(272, 280); rcp = sp(280, 288)
    rc2 = sp(288, 296); sc2 = sp(296, 304)
    q8t = sp(304, 368).bitcast(FP8)[:, 0:256]
    dqf_fm = sp(1792, 2048).rearrange("p (c t) -> p c t", t=128)
    gout = [sp(368, 624), sp(624, 880)]

    dq1g, dq1u, dq1d = g["dq1g_g"], g["dq1u_g"], g["dq1d_g"]
    gu_done = {0: 0, 1: 0}
    dw_done = {0: 0, 1: 0}
    psgu_done = {0: 0, 1: 0}
    tb_done = {2: 0, 3: 0}
    psd_done = {4: 0, 5: 0}
    gout_done = {0: 0, 1: 0}
    dqf_read = 0
    dpart_done = 0

    def load_gu(io, buf):
        rank = (io * 256) // 1024
        iloc = (io * 256) % 1024
        src_g = dq1g[rank * 2048:(rank + 1) * 2048, iloc:iloc + 256]
        src_u = dq1u[rank * 2048:(rank + 1) * 2048, iloc:iloc + 256]
        C.wait("sync", "t_" + pre, gu_done[buf])
        C.dma("sync", gw[buf], src_g.rearrange("(c p) i -> p c i", p=128),
              "lwg%d_" % buf + pre)
        C.dma("sync", uw[buf], src_u.rearrange("(c p) i -> p c i", p=128),
              "lwg%d_" % buf + pre)

    def load_dw(io, buf):
        C.wait("sync", "t3_" + pre, dw_done[buf])
        for eb in range(8):
            src = dq1d[eb * 8192 + io * 256: eb * 8192 + (io + 1) * 256, :]
            C.dma("sync", dw[buf][:, :, eb * 256:(eb + 1) * 256],
                  src.rearrange("(c p) e -> p c e", p=128), "lwd%d_" % buf + pre)

    C.barrier()
    for tg in range(4):
        C.wait("vector", "s2_" + pre, dpart_done)
        C.op("vector", lambda e: e.memset(dpart[:, :, :], 0.0), outs=[dpart],
             inc="v0_" + pre)
        load_gu(0, 0)
        load_dw(0, 0)
        load_gu(1, 1)
        load_dw(1, 1)
        for io in range(32):
            buf = io % 2
            C.wait("tensor", "lwg%d_" % buf + pre, C.cnt["lwg%d_" % buf + pre])
            C.wait("tensor", "lwd%d_" % buf + pre, C.cnt["lwd%d_" % buf + pre])
            for u in range(2):
                t = tg * 2 + u
                psg, psu = PS[0], PS[1]
                C.wait("tensor", "s_" + pre, psgu_done[0])
                C.wait("tensor", "v_" + pre, psgu_done[1])
                for hc in range(16):
                    C.op("tensor", lambda e, hc=hc, t=t, buf=buf, psg=psg: e.matmul(
                        psg[:, 0:256], dqx_fm[:, hc, t * 128:(t + 1) * 128],
                        gw[buf][:, hc, :], start=(hc == 0), stop=(hc == 15)))
                for hc in range(16):
                    C.op("tensor", lambda e, hc=hc, t=t, buf=buf, psu=psu: e.matmul(
                        psu[:, 0:256], dqx_fm[:, hc, t * 128:(t + 1) * 128],
                        uw[buf][:, hc, :], start=(hc == 0), stop=(hc == 15)),
                        inc=("t_" + pre) if hc == 15 else None)
                C.wait("scalar", "t_" + pre, C.cnt["t_" + pre])
                C.op("scalar", lambda e, psg=psg: e.activation(
                    out=gate_t, in_=psg[:, 0:256], func=AF.Silu),
                    outs=[gate_t], inc="s_" + pre)
                psgu_done[0] = C.cnt["s_" + pre]
                C.wait("vector", "s_" + pre, C.cnt["s_" + pre])
                C.op("vector", lambda e, psu=psu: e.tensor_tensor(
                    fuse_t, gate_t, psu[:, 0:256], ALU.mult),
                    outs=[fuse_t], ins=[gate_t], inc="v_" + pre)
                psgu_done[1] = C.cnt["v_" + pre]
                fv = fuse_t.rearrange("p (b k) -> p b k", k=32)
                C.op("vector", lambda e, fv=fv: e.reduce_max(
                    amx, fv, axis=AX.X, apply_absolute_value=True),
                    outs=[amx], ins=[fuse_t])
                C.op("vector", lambda e: e.tensor_scalar(
                    scl, amx, 1.0 / 448.0, 1e-30, ALU.mult, ALU.max),
                    outs=[scl], ins=[amx])
                C.op("vector", lambda e: e.reciprocal(rcp, scl),
                     outs=[rcp], ins=[scl])
                C.op("vector", lambda e: e.tensor_scalar_mul(rc2, rcp, 0.5),
                     outs=[rc2], ins=[rcp])
                C.op("vector", lambda e: e.tensor_scalar_mul(sc2, scl, 2.0),
                     outs=[sc2], ins=[scl])
                q8v = q8t.rearrange("p (b k) -> p b k", k=32)
                C.op("vector", lambda e, q8v=q8v, fv=fv: e.tensor_tensor(
                    q8v, fv, rc2.unsqueeze(2).broadcast_to([128, 8, 32]), ALU.mult),
                    outs=[q8t], ins=[fuse_t, rc2])
                dqv = dqf_t.rearrange("p (b k) -> p b k", k=32)
                C.op("vector", lambda e, dqv=dqv, q8v=q8v: e.tensor_tensor(
                    dqv, q8v, sc2.unsqueeze(2).broadcast_to([128, 8, 32]), ALU.mult),
                    outs=[dqf_t], ins=[q8t, sc2], inc="v_" + pre)
                C.wait("tensor", "v_" + pre, C.cnt["v_" + pre])
                C.wait("tensor", "t3_" + pre, dqf_read)
                for cc in range(2):
                    bb = 2 + cc
                    pban = PS[bb]
                    C.wait("tensor", "sv%d_" % bb + pre, tb_done[bb])
                    C.op("tensor", lambda e, pban=pban, cc=cc: e.transpose(
                        pban[:, 0:128], dqf_t[:, cc * 128:(cc + 1) * 128], ident),
                        inc="t2_" + pre)
                    eng = "scalar" if cc == 0 else "vector"
                    C.wait(eng, "t2_" + pre, C.cnt["t2_" + pre])
                    if eng == "scalar":
                        C.op(eng, lambda e, pban=pban, cc=cc: e.copy(
                            out=dqf_fm[:, cc, :], in_=pban[:, 0:128]),
                            outs=[dqf_fm], inc="sv%d_" % bb + pre)
                    else:
                        C.op(eng, lambda e, pban=pban, cc=cc: e.tensor_copy(
                            dqf_fm[:, cc, :], pban[:, 0:128]),
                            outs=[dqf_fm], inc="sv%d_" % bb + pre)
                    tb_done[bb] = C.cnt["sv%d_" % bb + pre]
                C.wait("tensor", "sv2_" + pre, tb_done[2])
                C.wait("tensor", "sv3_" + pre, tb_done[3])
                for eb in range(4):
                    bank = 4 + eb % 2
                    psd = PS[bank]
                    C.wait("tensor", "v2_" + pre, psd_done[bank])
                    for cc in range(2):
                        C.op("tensor", lambda e, psd=psd, cc=cc, eb=eb, buf=buf: e.matmul(
                            psd[:, :], dqf_fm[:, cc, :],
                            dw[buf][:, cc, eb * 512:(eb + 1) * 512],
                            start=(cc == 0), stop=(cc == 1)),
                            inc=("t3_" + pre) if cc == 1 else None)
                    C.wait("vector", "t3_" + pre, C.cnt["t3_" + pre])
                    C.op("vector", lambda e, psd=psd, u=u, eb=eb: e.tensor_tensor(
                        dpart[:, u, eb * 512:(eb + 1) * 512],
                        dpart[:, u, eb * 512:(eb + 1) * 512], psd[:, :], ALU.add),
                        outs=[dpart], ins=[dpart], inc="v2_" + pre)
                    psd_done[bank] = C.cnt["v2_" + pre]
                dqf_read = C.cnt["t3_" + pre]
            gu_done[buf] = C.cnt["t_" + pre]
            dw_done[buf] = C.cnt["t3_" + pre]
            if io + 2 < 32:
                load_gu(io + 2, buf)
                load_dw(io + 2, buf)
        for u in range(2):
            t = tg * 2 + u
            for eb in range(8):
                sl = eb % 2
                C.wait("scalar", "v2_" + pre, C.cnt["v2_" + pre])
                C.wait("scalar", "gst%d_" % sl + pre, gout_done[sl])
                C.op("scalar", lambda e, u=u, eb=eb, sl=sl: e.activation(
                    out=gout[sl],
                    in_=dpart[:, u, eb * 256:(eb + 1) * 256], func=AF.Silu),
                    outs=[gout[sl]], ins=[dpart], inc="s2_" + pre)
                C.wait("gpsimd", "s2_" + pre, C.cnt["s2_" + pre])
                C.dma("gpsimd", g["gd"][t, :, eb * 256:(eb + 1) * 256], gout[sl],
                      "gst%d_" % sl + pre)
                gout_done[sl] = C.cnt["gst%d_" % sl + pre]
        dpart_done = C.cnt["s2_" + pre]
    C.barrier()


def emit_attn(C, nc, env, layer):
    g = env
    s1, s2, s3, sp = g["s1"], g["s2"], g["s3"], g["sp"]
    PS = g["PS"]
    ones2d = g["ones2d"]
    pre = "d"
    kg = g["k0g"] if layer == 0 else g["k1g"]
    vg = g["v0g"] if layer == 0 else g["v1g"]
    KVH = 4 if layer == 0 else 16
    ao_fm = s1(0, 16384).rearrange("p (h t) -> p h t", t=1024)

    masks = s2(0, 4096).bitcast(BF16).rearrange("p (m q) -> p m q", q=512)
    qrow = sp(264, 264 + 1024 // 1)
    qrow = s3(12288 - 1024, 12288)[0:1, :]
    C.dma("sync", qrow, g["qpos_row"], "ld_" + pre)
    qbc = s2(4096, 4608)
    kcol = s2(4608, 4609)
    kcoli = s2(4609, 4610)
    m01 = s2(4610, 5122)
    C.wait("tensor", "ld_" + pre, C.cnt["ld_" + pre])
    C.op("tensor", lambda e: e.matmul(PS[6][:, :], ones2d[0:1, :], qrow[:, 0:512],
                                      start=True, stop=True), inc="t_" + pre)
    C.wait("vector", "t_" + pre, C.cnt["t_" + pre])
    C.op("vector", lambda e: e.tensor_copy(qbc, PS[6][:, :]), outs=[qbc])
    for rr in range(4):
        for kt in range(4):
            mi = rr * 4 + kt
            C.wait("gpsimd", "v_" + pre, C.cnt.get("v_" + pre, 0))
            C.op("gpsimd", lambda e, rr=rr, kt=kt: e.iota(
                kcoli.bitcast(I32), pattern=[[0, 1]], base=rr + 4 * (kt * 128),
                channel_multiplier=4), outs=[kcoli], inc="g_" + pre)
            C.wait("vector", "g_" + pre, C.cnt["g_" + pre])
            C.op("vector", lambda e: e.tensor_copy(kcol, kcoli.bitcast(I32)),
                 outs=[kcol], ins=[kcoli])
            C.op("vector", lambda e: e.tensor_scalar(
                m01, qbc, kcol, None, ALU.is_ge),
                outs=[m01], ins=[qbc, kcol])
            C.op("vector", lambda e, mi=mi: e.tensor_scalar(
                masks[:, mi, :], m01, 1.0, -NEG_BIG, ALU.subtract, ALU.mult),
                outs=[masks], ins=[m01], inc="v_" + pre)

    kbuf = [s2(5632, 9728), s2(9728, 13824)]
    vbuf = [s2(13824 + 0, 13824 + 0) if False else None, None]
    vbuf = [s3(0, 4096).rearrange("p (c h) -> p c h", h=128),
            s3(4096, 8192).rearrange("p (c h) -> p c h", h=128)]
    qbuf = [s3(8192, 8704), s3(8704, 9216)]
    ptile = [s3(9216, 9728), s3(9728, 10240), s3(10240, 10752)]
    rsum = s3(10752, 11264)[0:1, :]
    rbc = s3(11264, 11776)

    C.barrier()

    def chunks_for(qt):
        out = []
        if qt == 0:
            for rr in range(4):
                out.append((rr, 0, True))
        else:
            for rr in range(4):
                out.append((rr, 0, False))
            for rr in range(4):
                out.append((rr, 1, True))
        return out

    pt_done = {}
    qld_done = {0: 0, 1: 0}
    kv_done = {0: 0, 1: 0}
    kvv_done = {0: 0, 1: 0}
    psS_done = {5: 0, 6: 0}
    for h in range(16):
        kvh = h // 4 if layer == 0 else h
        for qt in range(2):
            ch = chunks_for(qt)
            nch = len(ch)
            qsl = qt % 2
            qb = qbuf[qsl]
            C.wait("sync", "t2_" + pre, qld_done[qsl])
            C.dma("sync", qb, g["qd"][h, :, qt * 512:(qt + 1) * 512], "ldq%d_" % qsl + pre)

            def load_kv(ci, sl):
                rr, half, _ = ch[ci]
                ksrc = kg[rr * (KVH * 128) + kvh * 128: rr * (KVH * 128) + (kvh + 1) * 128,
                          half * 512:(half + 1) * 512]
                C.wait("sync", "t2_" + pre, kv_done[sl])
                C.dma("sync", kbuf[sl][:, 0:512], ksrc, "ldk%d_" % sl + pre)
                vsrc = vg[rr * 1024 + half * 512: rr * 1024 + (half + 1) * 512,
                          kvh * 128:(kvh + 1) * 128]
                C.wait("sync", "tav_" + pre, kvv_done[sl])
                C.dma("sync", vbuf[sl][:, 0:4, :],
                      vsrc.rearrange("(c p) h -> p c h", p=128), "ldv%d_" % sl + pre)

            for ci in range(min(2, nch)):
                load_kv(ci, ci % 2)
            pso = PS[2]
            pss = PS[3]
            psb = PS[4]
            C.wait("tensor", "ldq%d_" % qsl + pre, C.cnt["ldq%d_" % qsl + pre])
            njt = nch * 4
            exp_mark = {}

            def emit_score(j):
                ci, kt = j // 4, j % 4
                rr, half, masked = ch[ci]
                sl = ci % 2
                if kt == 0:
                    C.wait("tensor", "ldk%d_" % sl + pre, C.cnt["ldk%d_" % sl + pre])
                kb = kbuf[sl]
                sb = 5 + (j % 2)
                pss_k = PS[sb]
                C.wait("tensor", "s_" + pre, psS_done[sb])
                C.op("tensor", lambda e, kb=kb, kt=kt, qb=qb, pss_k=pss_k: e.matmul(
                    pss_k[:, :], kb[:, kt * 128:(kt + 1) * 128], qb,
                    start=True, stop=True), inc="t2_" + pre)
                if masked:
                    C.wait("vector", "t2_" + pre, C.cnt["t2_" + pre])
                    mi = rr * 4 + kt
                    C.op("vector", lambda e, pss_k=pss_k, mi=mi: e.tensor_tensor(
                        pss_k[:, :], pss_k[:, :], masks[:, mi, :], ALU.add),
                        ins=[masks], inc="v2_" + pre)
                    C.wait("scalar", "v2_" + pre, C.cnt["v2_" + pre])
                else:
                    C.wait("scalar", "t2_" + pre, C.cnt["t2_" + pre])
                pt = ptile[j % 3]
                C.wait("scalar", "tav_" + pre, pt_done.get(j % 3, 0))
                C.op("scalar", lambda e, pt=pt, pss_k=pss_k: e.activation(
                    out=pt, in_=pss_k[:, :], func=AF.Exp),
                    outs=[pt], inc="s_" + pre)
                psS_done[sb] = C.cnt["s_" + pre]
                exp_mark[j] = C.cnt["s_" + pre]

            def emit_av(j):
                ci, kt = j // 4, j % 4
                sl = ci % 2
                vb = vbuf[sl]
                if kt == 0:
                    C.wait("tensor", "ldv%d_" % sl + pre, C.cnt["ldv%d_" % sl + pre])
                pt = ptile[j % 3]
                C.wait("tensor", "s_" + pre, exp_mark[j])
                first = (j == 0)
                last = (j == njt - 1)
                C.op("tensor", lambda e, vb=vb, kt=kt, pt=pt, pso=pso, first=first,
                     last=last: e.matmul(
                    pso[:, :], vb[:, kt, :], pt, start=first, stop=last))
                C.op("tensor", lambda e, pt=pt, pss=pss, first=first, last=last: e.matmul(
                    pss[0:1, :], ones2d[:, 0:1], pt, start=first, stop=last),
                    inc="tav_" + pre)
                pt_done[j % 3] = C.cnt["tav_" + pre]
                if kt == 3:
                    kv_done[sl] = C.cnt["t2_" + pre]
                    kvv_done[sl] = C.cnt["tav_" + pre]
                    if ci + 2 < nch:
                        load_kv(ci + 2, sl)

            emit_score(0)
            for j in range(njt):
                if j + 1 < njt:
                    emit_score(j + 1)
                emit_av(j)
            qld_done[qsl] = C.cnt["t2_" + pre]
            C.wait("vector", "tav_" + pre, C.cnt["tav_" + pre])
            C.op("vector", lambda e, pss=pss: e.reciprocal(rsum, pss[0:1, :]),
                 outs=[rsum], inc="v3_" + pre)
            C.wait("tensor", "v3_" + pre, C.cnt["v3_" + pre])
            C.op("tensor", lambda e, psb=psb: e.matmul(
                psb[:, :], ones2d[0:1, :], rsum, start=True, stop=True),
                inc="t4_" + pre)
            C.wait("scalar", "t4_" + pre, C.cnt["t4_" + pre])
            C.wait("scalar", "v4_" + pre, C.cnt.get("v4_" + pre, 0))
            C.op("scalar", lambda e, psb=psb: e.copy(out=rbc, in_=psb[:, :]),
                 outs=[rbc], inc="s4_" + pre)
            C.wait("vector", "s4_" + pre, C.cnt["s4_" + pre])
            C.op("vector", lambda e, pso=pso, h=h, qt=qt: e.tensor_tensor(
                ao_fm[:, h, qt * 512:(qt + 1) * 512], pso[:, :], rbc, ALU.mult),
                outs=[ao_fm], ins=[rbc], inc="v4_" + pre)
            C.wait("tensor", "v4_" + pre, C.cnt["v4_" + pre])
    C.barrier()


def emit_phase_E(C, nc, env, layer):
    g = env
    s1, s2, s3, sp = g["s1"], g["s2"], g["s3"], g["sp"]
    PS = g["PS"]
    ones2d = g["ones2d"]
    pre = "e"
    ao_fm = s1(0, 16384).rearrange("p (h t) -> p h t", t=1024)
    wt = g["w0o_t"] if layer == 0 else g["w1o_t"]
    bo = g["b0o"] if layer == 0 else g["b1o"]
    x_src = g["x_in"] if layer == 0 else g["x0d"]
    x_dst = g["x0d"] if layer == 0 else g["y_out"]

    brow = s2(8192, 10240)[0:1, :]
    C.dma("sync", brow, bo[:, :], "ldpro_" + pre)
    wstage = [s2(0, 4096).rearrange("p (c o) -> p c o", o=256),
              s2(4096, 8192).rearrange("p (c o) -> p c o", o=256)]
    gt = [s3(0, 2048), s3(2048, 4096)]
    xt = [s3(4096, 6144), s3(6144, 8192)]
    xo = [s3(8192, 10240), s3(10240, 12288)]

    NE = 8
    mm_done = {}
    evac_done = {}
    tile_done = {}

    def load_w(j):
        src_w = wt.rearrange("(c p) o -> p c o", p=128)[:, :, (j % NE) * 256:(j % NE) * 256 + 256]
        C.wait("sync", "t_" + pre, mm_done.get(j - 2, 0))
        C.dma("sync", wstage[j % 2], src_w, "lw%d_" % (j % 2) + pre)

    load_w(0)
    load_w(1)
    C.wait("tensor", "ldpro_" + pre, C.cnt["ldpro_" + pre])
    for t in range(NT):
        sl = t % 2
        if t >= 2:
            C.wait("sync", "v_" + pre, tile_done[t - 2][1])
        C.dma("sync", gt[sl], g["gd"][t, :, :], "lg%d_" % sl + pre)
        C.dma("sync", xt[sl], x_src[t * 128:(t + 1) * 128, :], "lg%d_" % sl + pre)
        for ec in range(NE):
            j = t * NE + ec
            wbuf = wstage[j % 2]
            C.wait("tensor", "lw%d_" % (j % 2) + pre, C.cnt["lw%d_" % (j % 2) + pre])
            bank = j % 2
            ps = PS[bank]
            C.wait("tensor", "v_" + pre, evac_done.get(bank, 0))
            for dc in range(16):
                C.op("tensor", lambda e, ps=ps, dc=dc, t=t, wbuf=wbuf: e.matmul(
                    ps[:, 0:256], ao_fm[:, dc, t * 128:(t + 1) * 128], wbuf[:, dc, :],
                    start=(dc == 0), stop=False))
            C.op("tensor", lambda e, ps=ps, ec=ec: e.matmul(
                ps[:, 0:256], ones2d[0:1, :], brow[:, ec * 256:(ec + 1) * 256],
                start=False, stop=True), inc="t_" + pre)
            mm_done[j] = C.cnt["t_" + pre]
            if j + 2 < NT * NE:
                load_w(j + 2)
            C.wait("vector", "t_" + pre, C.cnt["t_" + pre])
            C.wait("vector", "lg%d_" % sl + pre, C.cnt["lg%d_" % sl + pre])
            if t >= 2 and ec == 0:
                C.wait("vector", "gst%d_" % sl + pre, tile_done[t - 2][0])
            C.op("vector", lambda e, ps=ps, sl=sl, ec=ec: e.tensor_tensor(
                xo[sl][:, ec * 256:(ec + 1) * 256], ps[:, 0:256],
                gt[sl][:, ec * 256:(ec + 1) * 256], ALU.mult),
                outs=[xo[sl]], ins=[gt[sl]], inc="v_" + pre)
            evac_done[bank] = C.cnt["v_" + pre]
            C.op("vector", lambda e, sl=sl, ec=ec: e.tensor_tensor(
                xo[sl][:, ec * 256:(ec + 1) * 256],
                xo[sl][:, ec * 256:(ec + 1) * 256],
                xt[sl][:, ec * 256:(ec + 1) * 256], ALU.add),
                outs=[xo[sl]], ins=[xo[sl], xt[sl]], inc="v_" + pre)
        C.wait("gpsimd", "v_" + pre, C.cnt["v_" + pre])
        C.dma("gpsimd", x_dst[t * 128:(t + 1) * 128, :], xo[sl], "gst%d_" % sl + pre)
        tile_done[t] = (C.cnt["gst%d_" % sl + pre], C.cnt["v_" + pre], C.cnt["v_" + pre])
    C.barrier()



def stage_inputs(inputs):
    f32 = np.float32
    x = np.ascontiguousarray(inputs["L0_attn_x"], dtype=f32)
    tr = lambda a: np.ascontiguousarray(np.asarray(a, dtype=f32).T)
    row = lambda a: np.ascontiguousarray(np.asarray(a, dtype=f32).reshape(1, -1))
    shared = dict(
        w0q_t=tr(inputs["L0_attn_Wq"]), w0k_t=tr(inputs["L0_attn_Wk"]),
        w0v_t=tr(inputs["L0_attn_Wv"]), w0o_t=tr(inputs["L0_attn_Wo"]),
        b0q=row(inputs["L0_attn_bq"]), b0k=row(inputs["L0_attn_bk"]),
        b0v=row(inputs["L0_attn_bv"]), b0o=row(inputs["L0_attn_bo"]),
        gain0=np.ascontiguousarray(
            np.broadcast_to(np.asarray(inputs["L0_attn_q_gain"], f32)[None, :],
                            (128, 16))),
        gs0=np.ascontiguousarray(
            np.asarray(inputs["L0_ffn_gate_scale"], f32).reshape(48, 128).T),
        us0=np.ascontiguousarray(
            np.asarray(inputs["L0_ffn_up_scale"], f32).reshape(48, 128).T),
        ds0bc=np.ascontiguousarray(
            np.broadcast_to(np.asarray(inputs["L0_ffn_down_scale"], f32)[None, :],
                            (128, 2048))),
        w1q_t=tr(inputs["L1_attn_Wq"]), w1k_t=tr(inputs["L1_attn_Wk"]),
        w1v_t=tr(inputs["L1_attn_Wv"]), w1o_t=tr(inputs["L1_attn_Wo"]),
        b1q=row(inputs["L1_attn_bq"]), b1k=row(inputs["L1_attn_bk"]),
        b1v=row(inputs["L1_attn_bv"]), b1o=row(inputs["L1_attn_bo"]),
    )
    g0 = np.ascontiguousarray(np.asarray(inputs["L0_ffn_gate_w"], f32))
    u0 = np.ascontiguousarray(np.asarray(inputs["L0_ffn_up_w"], f32))
    d0 = np.ascontiguousarray(np.asarray(inputs["L0_ffn_down_w"], f32))
    g1 = np.ascontiguousarray(np.asarray(inputs["L1_ffn_gate_w"], f32))
    u1 = np.ascontiguousarray(np.asarray(inputs["L1_ffn_up_w"], f32))
    d1 = np.ascontiguousarray(np.asarray(inputs["L1_ffn_down_w"], f32))

    in_maps = []
    for c in range(NCORES):
        gidx, r = c // RPG, c % RPG
        pos = (r + 4 * np.arange(TLOC)).astype(f32)
        m = dict(shared)
        m["x_in"] = np.ascontiguousarray(x[gidx, r::4, :])
        m["qpos_cols"] = np.ascontiguousarray(pos.reshape(NT, 128).T)
        m["qpos_row"] = np.ascontiguousarray(pos.reshape(1, TLOC))
        m["p0g"] = np.ascontiguousarray(g0[c * 256:(c + 1) * 256, :])
        m["p0u"] = np.ascontiguousarray(u0[c * 256:(c + 1) * 256, :])
        m["p0d"] = np.ascontiguousarray(d0[c * 768:(c + 1) * 768, :])
        m["p1g"] = np.ascontiguousarray(g1[c * 1024:(c + 1) * 1024, :])
        m["p1u"] = np.ascontiguousarray(u1[c * 1024:(c + 1) * 1024, :])
        m["p1d"] = np.ascontiguousarray(d1[c * 256:(c + 1) * 256, :])
        in_maps.append(m)
    return in_maps


_NC_CACHE = {}


def kernel(**inputs):
    key = "full"
    if key not in _NC_CACHE:
        _NC_CACHE[key] = build_program()
    nc, _ = _NC_CACHE[key]
    in_maps = stage_inputs(inputs)
    res = run_bass_kernel_spmd(nc, in_maps, core_ids=list(range(NCORES)))
    out = np.empty((B, S, D), dtype=np.float32)
    for c in range(NCORES):
        gidx, r = c // RPG, c % RPG
        out[gidx, r::4, :] = res.results[c]["y"]
    return out


# version 10 (final)
# speedup vs baseline: 1.0483x; 1.0483x over previous
import numpy as np
from contextlib import ExitStack

import concourse.bass as bass
import concourse.mybir as mybir
from concourse.bass_utils import run_bass_kernel_spmd

F32 = mybir.dt.float32
BF16 = mybir.dt.bfloat16
FP8 = mybir.dt.float8e4
I32 = mybir.dt.int32
AF = mybir.ActivationFunctionType
ALU = mybir.AluOpType
AX = mybir.AxisListType

NCORES = 8
RPG = 4
B, S, D = 2, 4096, 2048
TLOC = 1024
NT = 8
RG = [[0, 1, 2, 3], [4, 5, 6, 7]]
NEG_BIG = -1.0e30
LN_EPS = 1e-5
RMS_EPS = 1.1920929e-07
ROPE_BASE = 10000.0
TWO_PI = float(2.0 * np.pi)
INV_2PI = float(1.0 / TWO_PI)
CW1 = float(np.float32(6.28125))
CW2 = float(np.float32(2.0 * np.pi - 6.28125))
CW3 = float(2.0 * np.pi - 6.28125 - float(np.float32(2.0 * np.pi - 6.28125)))
PIO2_HI = float(np.float32(np.pi / 2))
PIO2_LO = float(np.pi / 2 - float(np.float32(np.pi / 2)))
SCALE_HD = float(1.0 / np.sqrt(128.0))

ENGS = ("sync", "tensor", "vector", "scalar", "gpsimd")


class Ctx:

    def __init__(self, nc):
        self.nc = nc
        self.items = []
        self.sems = {}
        self.cnt = {}
        self.dirty = {e: set() for e in ENGS}
        self._es = None

    def sem(self, name):
        if name not in self.sems:
            h = self._es.enter_context(self.nc.semaphore(name))
            self.sems[name] = h
            self.cnt[name] = 0
        return self.sems[name]

    def _names(self, aps):
        out = []
        for a in aps:
            t = getattr(a, "tensor", None)
            if t is not None:
                out.append(t.name)
        return out

    def op(self, eng, emit, outs=(), ins=(), inc=None, inc_by=1):
        innames = self._names(ins)
        d = self.dirty[eng]
        if any(n in d for n in innames):
            self.items.append((eng, lambda e: e.drain()))
            d.clear()
        for n in self._names(outs):
            d.add(n)
        val = None
        if inc is not None:
            self.sem(inc)
            self.cnt[inc] += inc_by
            val = self.cnt[inc]
            semh = self.sems[inc]

            def fn(e, emit=emit, semh=semh, by=inc_by):
                r = emit(e)
                r.then_inc(semh, by)
            self.items.append((eng, fn))
        else:
            self.items.append((eng, emit))
        return val

    def dma(self, eng, out, in_, inc):
        return self.op(
            eng, lambda e, o=out, i=in_: e.dma_start(out=o, in_=i),
            outs=(), ins=(), inc=inc, inc_by=16)

    def wait(self, eng, sem, val):
        semh = self.sem(sem)
        self.items.append((eng, lambda e, s=semh, v=val: e.wait_ge(s, v)))

    def wait_cur(self, eng, sem):
        self.wait(eng, sem, self.cnt.get(sem, 0))

    def barrier(self, engs=ENGS, sems=None):
        names = sems if sems is not None else [n for n in self.cnt if not n.startswith("cc")]
        for e in engs:
            for s in names:
                v = self.cnt.get(s, 0)
                if v > 0:
                    self.wait(e, s, v)

    def replay(self, eng_name, engobj):
        for e, fn in self.items:
            if e == eng_name:
                fn(engobj)


def ceil_div(a, b):
    return (a + b - 1) // b



def build_program(upto="E1", taps=()):
    nc = bass.Bass("TRN2", target_bir_lowering=False)
    ph_order = ["PREP", "A0", "B0", "G0", "C0", "D0", "E0",
                "A1", "B1", "G1", "C1", "D1", "E1"]
    assert upto in ph_order
    lim = ph_order.index(upto)

    def on(ph):
        return ph_order.index(ph) <= lim

    def din(name, shape, dt=F32):
        return nc.dram_tensor(name, shape, dt, kind="ExternalInput").ap()

    def dint(name, shape, dt=F32):
        return nc.dram_tensor(name, shape, dt).ap()

    x_in = din("x_in", [TLOC, D])
    qpos_cols = din("qpos_cols", [128, NT])
    qpos_row = din("qpos_row", [1, TLOC])
    w0q_t = din("w0q_t", [D, 2048]); w0k_t = din("w0k_t", [D, 512])
    w0v_t = din("w0v_t", [D, 512]);  w0o_t = din("w0o_t", [D, 2048])
    b0q = din("b0q", [1, 2048]); b0k = din("b0k", [1, 512])
    b0v = din("b0v", [1, 512]);  b0o = din("b0o", [1, 2048])
    gain0 = din("gain0", [128, 16])
    p0g = din("p0g", [256, 6144]); p0u = din("p0u", [256, 6144])
    p0d = din("p0d", [768, 2048])
    gs0 = din("gs0", [128, 48]); us0 = din("us0", [128, 48])
    ds0bc = din("ds0bc", [128, 2048])
    w1q_t = din("w1q_t", [D, 2048]); w1k_t = din("w1k_t", [D, 2048])
    w1v_t = din("w1v_t", [D, 2048]); w1o_t = din("w1o_t", [D, 2048])
    b1q = din("b1q", [1, 2048]); b1k = din("b1k", [1, 2048])
    b1v = din("b1v", [1, 2048]); b1o = din("b1o", [1, 2048])
    p1g = din("p1g", [1024, 2048]); p1u = din("p1u", [1024, 2048])
    p1d = din("p1d", [256, 8192])

    y_out = nc.dram_tensor("y", [TLOC, D], F32, kind="ExternalOutput").ap()

    x0d = dint("x0d", [TLOC, D])
    xnd = dint("xnd", [TLOC, D])
    qd = dint("qd", [16, 128, TLOC])
    gd = dint("gd", [NT, 128, D])
    k0s = dint("k0s", [512, TLOC]); k0g = dint("k0g", [RPG * 512, TLOC])
    v0s = dint("v0s", [TLOC, 512]); v0g = dint("v0g", [RPG * TLOC, 512])
    k1s = dint("k1s", [2048, TLOC]); k1g = dint("k1g", [RPG * 2048, TLOC])
    v1s = dint("v1s", [TLOC, 2048]); v1g = dint("v1g", [RPG * TLOC, 2048])
    q0g_s = dint("q0g_s", [256, 6144], FP8); q0g_g = dint("q0g_g", [2048, 6144], FP8)
    q0u_s = dint("q0u_s", [256, 6144], FP8); q0u_g = dint("q0u_g", [2048, 6144], FP8)
    q0d_s = dint("q0d_s", [768, 2048], FP8); q0d_g = dint("q0d_g", [6144, 2048], FP8)
    dq1g_s = dint("dq1g_s", [2048, 1024]); dq1g_g = dint("dq1g_g", [8 * 2048, 1024])
    dq1u_s = dint("dq1u_s", [2048, 1024]); dq1u_g = dint("dq1u_g", [8 * 2048, 1024])
    dq1d_s = dint("dq1d_s", [8192, 256]);  dq1d_g = dint("dq1d_g", [8 * 8192, 256])

    tap_outs = {}
    dram_by_name = dict(x0d=x0d, xnd=xnd, qd=qd, gd=gd, k0g=k0g, v0g=v0g, k1g=k1g, v1g=v1g,
                        q0g_g=q0g_g, q0d_g=q0d_g, dq1g_g=dq1g_g, dq1d_g=dq1d_g,
                        k0s=k0s, v0s=v0s, k1s=k1s, v1s=v1s, dq1g_s=dq1g_s,
                        dq1d_s=dq1d_s, q0g_s=q0g_s)
    for t in taps:
        if t.startswith("SBUF"):
            w = 2048 if t == "SBUFP" else (12288 if t == "SBUF3" else 16384)
            tap_outs[t] = nc.dram_tensor(
                "tap_" + t, [128, w], F32, kind="ExternalOutput").ap()
            continue
        src = dram_by_name[t]
        tap_outs[t] = nc.dram_tensor(
            "tap_" + t, list(src.shape), src.dtype, kind="ExternalOutput").ap()

    es = ExitStack()
    ctx = Ctx(nc)
    ctx._es = es
    with es:
        S1 = es.enter_context(nc.sbuf_tensor("S1", [128, 16384], F32))
        S2 = es.enter_context(nc.sbuf_tensor("S2", [128, 16384], F32))
        S3 = es.enter_context(nc.sbuf_tensor("S3", [128, 12288], F32))
        SP = es.enter_context(nc.sbuf_tensor("SP", [128, 2048], F32))
        PS = [es.enter_context(nc.psum_tensor("PS%d" % i, [128, 512], F32))
              for i in range(8)]
        blk = es.enter_context(nc.Block())

        def s1(a, b):
            return S1[:, a:b]

        def s2(a, b):
            return S2[:, a:b]

        def s3(a, b):
            return S3[:, a:b]

        def sp(a, b):
            return SP[:, a:b]

        ident = SP[:, 0:128]
        ones2d = SP[:, 128:256]
        qpc = SP[:, 256:264]
        ropec = SP[:, 1024:1536].rearrange("p (t f) -> p t f", f=64)
        ropes = SP[:, 1536:2048].rearrange("p (t f) -> p t f", f=64)

        C = ctx

        if on("PREP"):
            emit_prep(C, nc, locals())
        if on("A0"):
            emit_phase_A(C, nc, locals(), 0)
        if on("B0"):
            emit_phase_B(C, nc, locals(), 0)
        if on("G0"):
            emit_gather(C, nc, locals(), 0)
        if on("C0"):
            emit_ffn0(C, nc, locals())
        if on("D0"):
            emit_attn(C, nc, locals(), 0)
        if on("E0"):
            emit_phase_E(C, nc, locals(), 0)
        if on("A1"):
            emit_phase_A(C, nc, locals(), 1)
        if on("B1"):
            emit_phase_B(C, nc, locals(), 1)
        if on("G1"):
            emit_gather(C, nc, locals(), 1)
        if on("C1"):
            emit_ffn1(C, nc, locals())
        if on("D1"):
            emit_attn(C, nc, locals(), 1)
        if on("E1"):
            emit_phase_E(C, nc, locals(), 1)

        C.barrier()
        for t, dst in tap_outs.items():
            if t.startswith("SBUF"):
                slab = {"SBUF1": S1, "SBUF2": S2, "SBUF3": S3, "SBUFP": SP}[t]
                C.dma("gpsimd", dst, slab[:, :], "st_tap")
                continue
            src = dram_by_name[t]
            flat_s = src.flatten()
            flat_d = dst.flatten()
            n = flat_s.shape[0]
            step = 1 << 20
            for off in range(0, n, step):
                sl = slice(off, min(off + step, n))
                C.dma("gpsimd", flat_d[sl], flat_s[sl], "st_tap")

        C.barrier(engs=("gpsimd",))

        @blk.sync
        def _(e):
            ctx.replay("sync", e)

        @blk.tensor
        def _(e):
            ctx.replay("tensor", e)

        @blk.vector
        def _(e):
            ctx.replay("vector", e)

        @blk.scalar
        def _(e):
            ctx.replay("scalar", e)

        @blk.gpsimd
        def _(e):
            ctx.replay("gpsimd", e)

    return nc, tap_outs



def emit_prep(C, nc, env):
    g = env
    ident, ones2d, qpc = g["ident"], g["ones2d"], g["qpc"]
    ropec, ropes = g["ropec"], g["ropes"]
    s1, s2, s3, sp = g["s1"], g["s2"], g["s3"], g["sp"]
    PS = g["PS"]

    for slab in (g["S1"], g["S2"], g["S3"], g["SP"]):
        sv = slab[:, :]
        C.op("gpsimd", lambda e, sv=sv: e.memset(sv, 0.0), outs=[sv],
             inc="g_prep")
    C.barrier()

    C.op("gpsimd", lambda e: e.memset(ident, 0.0), outs=[ident], inc="g_prep")
    C.wait("gpsimd", "g_prep", C.cnt["g_prep"])
    C.op("gpsimd", lambda e: e.affine_select(
        out=ident, in_=ident, compare_op=ALU.not_equal, fill=1.0,
        base=0, pattern=[[-1, 128]], channel_multiplier=1),
        outs=[ident], ins=[ident], inc="g_prep")
    C.op("gpsimd", lambda e: e.memset(ones2d, 1.0), outs=[ones2d], inc="g_prep")
    C.dma("sync", qpc, g["qpos_cols"], "ld_prep")

    invf_i = sp(272, 336)
    bmask = sp(336, 400)
    bflt = sp(400, 464)
    invf_bc = sp(464, 528)
    C.op("gpsimd", lambda e: e.iota(
        invf_i.bitcast(I32), pattern=[[1, 64]], base=0,
        channel_multiplier=0), outs=[invf_i], inc="g_prep")
    C.wait("vector", "g_prep", C.cnt["g_prep"])
    C.wait("vector", "ld_prep", C.cnt["ld_prep"])
    C.op("vector", lambda e: e.memset(invf_bc, 1.0), outs=[invf_bc])
    for k in range(6):
        ck = float(np.float32(10000.0 ** (-(2 ** k) / 64.0)))
        C.op("vector", lambda e, k=k: e.tensor_scalar(
            bmask.bitcast(I32), invf_i.bitcast(I32), 1 << k, None,
            ALU.bitwise_and), outs=[bmask], ins=[invf_i])
        C.op("vector", lambda e: e.tensor_copy(bflt, bmask.bitcast(I32)),
             outs=[bflt], ins=[bmask])
        C.op("vector", lambda e: e.tensor_scalar(bflt, bflt, 0.5, None, ALU.is_gt),
             outs=[bflt], ins=[bflt])
        C.op("vector", lambda e, ck=ck: e.tensor_scalar(
            bflt, bflt, ck - 1.0, 1.0, ALU.mult, ALU.add),
            outs=[bflt], ins=[bflt])
        if k < 5:
            C.op("vector", lambda e: e.tensor_tensor(invf_bc, invf_bc, bflt, ALU.mult),
                 outs=[invf_bc], ins=[invf_bc, bflt])
        else:
            C.op("vector", lambda e: e.tensor_tensor(invf_bc, invf_bc, bflt, ALU.mult),
                 outs=[invf_bc], ins=[invf_bc, bflt], inc="v_prep")

    frq = sp(528, 592)
    u = sp(592, 656)
    nf = sp(656, 720)
    red = sp(720, 784)
    ni = sp(784, 848)
    m01 = sp(848, 912)
    red2 = sp(912, 976)

    def cascade(dst, x_ap, k_ap):
        C.op("vector", lambda e: e.scalar_tensor_tensor(
            out=red2, in0=k_ap, scalar=-CW1, in1=x_ap, op0=ALU.mult, op1=ALU.add),
            outs=[red2], ins=[k_ap, x_ap])
        C.op("vector", lambda e: e.scalar_tensor_tensor(
            out=dst, in0=k_ap, scalar=-CW2, in1=red2, op0=ALU.mult, op1=ALU.add),
            outs=[dst], ins=[k_ap, red2])
        C.op("vector", lambda e: e.scalar_tensor_tensor(
            out=red2, in0=k_ap, scalar=-CW3, in1=dst, op0=ALU.mult, op1=ALU.add),
            outs=[red2], ins=[k_ap, dst])
        C.op("vector", lambda e: e.tensor_copy(dst, red2), outs=[dst], ins=[red2])
    PI_F32 = float(np.float32(np.pi))
    PI_CLAMP = float(np.float32(3.1415925))

    def range_reduce_and_sin(dst, add_pio2):
        if add_pio2:
            C.op("vector", lambda e: e.tensor_scalar(
                u, frq, INV_2PI, 0.25, ALU.mult, ALU.add), outs=[u], ins=[frq])
        else:
            C.op("vector", lambda e: e.tensor_scalar_mul(u, frq, INV_2PI),
                 outs=[u], ins=[frq])
        C.op("vector", lambda e: e.tensor_copy(ni.bitcast(I32), u),
             outs=[ni], ins=[u])
        C.op("vector", lambda e: e.tensor_copy(nf, ni.bitcast(I32)),
             outs=[nf], ins=[ni])
        cascade(red, frq, nf)
        if add_pio2:
            C.op("vector", lambda e: e.tensor_scalar(
                red, red, PIO2_HI, PIO2_LO, ALU.add, ALU.add),
                outs=[red], ins=[red])
        C.op("vector", lambda e: e.tensor_scalar(m01, red, PI_F32, None, ALU.is_gt),
             outs=[m01], ins=[red])
        cascade(red, red, m01)
        C.op("vector", lambda e: e.tensor_scalar(
            red, red, PI_CLAMP, -PI_CLAMP, ALU.min, ALU.max),
            outs=[red], ins=[red], inc="v_prep")
        C.wait("scalar", "v_prep", C.cnt["v_prep"])
        C.op("scalar", lambda e, dst=dst: e.activation(out=dst, in_=red, func=AF.Sin),
             outs=[dst], ins=[red], inc="s_prep")

    for t in range(NT):
        C.wait("vector", "s_prep", C.cnt.get("s_prep", 0))
        C.op("vector", lambda e, t=t: e.tensor_scalar(
            frq, invf_bc, qpc[:, t:t + 1], None, ALU.mult),
            outs=[frq], ins=[invf_bc, qpc])
        range_reduce_and_sin(ropes[:, t, :], add_pio2=False)
        C.wait("vector", "s_prep", C.cnt.get("s_prep", 0))
        range_reduce_and_sin(ropec[:, t, :], add_pio2=True)

    q8ctr = [0, 0]
    for mi, (src, dst_s, dst_g, rows, cols) in enumerate((
            (g["p0g"], g["q0g_s"], g["q0g_g"], 256, 6144),
            (g["p0u"], g["q0u_s"], g["q0u_g"], 256, 6144),
            (g["p0d"], g["q0d_s"], g["q0d_g"], 768, 2048))):
        nch = rows // 128
        for chm in range(nch):
            sl = (mi * 8 + chm) % 2
            buf = s1(0, 8192) if sl == 0 else s1(8192, 16384)
            buf = buf[:, 0:cols]
            q8 = (s2(0, 2048) if sl == 0 else s2(2048, 4096))
            q8 = q8.bitcast(FP8)[:, 0:cols]
            lsem = "ldp8_%d" % sl
            vsem = "vp8_%d" % sl
            ssem = "stp8_%d" % sl
            C.wait("sync", vsem, C.cnt.get(vsem, 0))
            C.dma("sync", buf, src[chm * 128:(chm + 1) * 128, :], lsem)
            C.wait("vector", lsem, C.cnt[lsem])
            C.wait("vector", ssem, C.cnt.get(ssem, 0))
            C.op("vector", lambda e, q8=q8, buf=buf: e.tensor_copy(q8, buf),
                 outs=[q8], ins=[buf], inc=vsem)
            C.wait("gpsimd", vsem, C.cnt[vsem])
            C.dma("gpsimd", dst_s[chm * 128:(chm + 1) * 128, :], q8, ssem)
        C.wait("gpsimd", "stp8_0", C.cnt.get("stp8_0", 0))
        C.wait("gpsimd", "stp8_1", C.cnt.get("stp8_1", 0))
        C.op("gpsimd", lambda e, dst_s=dst_s, dst_g=dst_g: e.collective_compute(
            "AllGather", ALU.bypass, replica_groups=[list(range(8))],
            ins=[dst_s.opt()], outs=[dst_g.opt()]), inc="cc")

    def mx_dequant_transpose(src, dst_s, dst_g, rows, cols, mslot):
        nch = rows // 128
        nblk = cols // 32
        for chm in range(nch):
            sl = (mslot * 8 + chm) % 2
            buf = (s1(0, 8192) if sl == 0 else s1(8192, 16384))[:, 0:cols]
            dq = (s2(0, 8192) if sl == 0 else s2(8192, 16384))[:, 0:cols]
            amx = s3(0, nblk)
            scl = s3(256, 256 + nblk)
            rcp = s3(512, 512 + nblk)
            rc2 = s3(768, 768 + nblk)
            sc2 = s3(1024, 1024 + nblk)
            q8t = s3(1280, 1280 + ceil_div(cols, 4)).bitcast(FP8)[:, 0:cols]
            lsem = "ldmx_%d" % sl
            vsem = "vmx_%d" % sl
            C.wait("sync", "tmx_%d" % sl, C.cnt.get("tmx_%d" % sl, 0))
            C.dma("sync", buf, src[chm * 128:(chm + 1) * 128, :], lsem)
            C.wait("vector", lsem, C.cnt[lsem])
            bufv = buf.rearrange("p (b k) -> p b k", k=32)
            C.op("vector", lambda e, amx=amx, bufv=bufv: e.reduce_max(
                amx, bufv, axis=AX.X, apply_absolute_value=True),
                outs=[amx], ins=[buf])
            C.op("vector", lambda e, scl=scl, amx=amx: e.tensor_scalar(
                scl, amx, 1.0 / 448.0, 1e-30, ALU.mult, ALU.max),
                outs=[scl], ins=[amx])
            C.op("vector", lambda e, rcp=rcp, scl=scl: e.reciprocal(rcp, scl),
                 outs=[rcp], ins=[scl])
            C.op("vector", lambda e, rc2=rc2, rcp=rcp: e.tensor_scalar_mul(rc2, rcp, 0.5),
                 outs=[rc2], ins=[rcp])
            C.op("vector", lambda e, sc2=sc2, scl=scl: e.tensor_scalar_mul(sc2, scl, 2.0),
                 outs=[sc2], ins=[scl])
            q8v = q8t.rearrange("p (b k) -> p b k", k=32)
            C.op("vector", lambda e, q8v=q8v, bufv=bufv, rc2=rc2, nblk=nblk: e.tensor_tensor(
                q8v, bufv, rc2[:, 0:nblk].unsqueeze(2).broadcast_to([128, nblk, 32]), ALU.mult),
                outs=[q8t], ins=[buf, rc2])
            dqv = dq.rearrange("p (b k) -> p b k", k=32)
            C.op("vector", lambda e, dqv=dqv, q8v=q8v, sc2=sc2, nblk=nblk: e.tensor_tensor(
                dqv, q8v, sc2[:, 0:nblk].unsqueeze(2).broadcast_to([128, nblk, 32]), ALU.mult),
                outs=[dq], ins=[q8t, sc2], inc=vsem)
            C.wait("tensor", vsem, C.cnt[vsem])
            ncc = cols // 128
            for cc in range(ncc):
                psl = cc % 2
                pban = PS[psl]
                C.wait("tensor", "smx_%d" % psl, C.cnt.get("smx_%d" % psl, 0))
                C.op("tensor", lambda e, pban=pban, dq=dq, cc=cc: e.transpose(
                    pban[:, 0:128], dq[:, cc * 128:(cc + 1) * 128], g["ident"]),
                    inc="tmx_%d" % sl)
                C.wait("scalar", "tmx_%d" % sl, C.cnt["tmx_%d" % sl])
                ev = s3(4096 + psl * 128, 4096 + psl * 128 + 128)
                C.wait("scalar", "stmx_%d" % psl, C.cnt.get("stmx_%d" % psl, 0))
                C.op("scalar", lambda e, ev=ev, pban=pban: e.copy(out=ev, in_=pban[:, 0:128]),
                     outs=[ev], inc="smx_%d" % psl)
                C.wait("gpsimd", "smx_%d" % psl, C.cnt["smx_%d" % psl])
                C.dma("gpsimd", dst_s[cc * 128:(cc + 1) * 128,
                                      chm * 128:(chm + 1) * 128], ev, "stmx_%d" % psl)
        C.wait("gpsimd", "stmx_0", C.cnt.get("stmx_0", 0))
        C.wait("gpsimd", "stmx_1", C.cnt.get("stmx_1", 0))
        C.op("gpsimd", lambda e, dst_s=dst_s, dst_g=dst_g: e.collective_compute(
            "AllGather", ALU.bypass, replica_groups=[list(range(8))],
            ins=[dst_s.opt()], outs=[dst_g.opt()]), inc="cc")

    C.barrier()
    mx_dequant_transpose(g["p1g"], g["dq1g_s"], g["dq1g_g"], 1024, 2048, 0)
    mx_dequant_transpose(g["p1u"], g["dq1u_s"], g["dq1u_g"], 1024, 2048, 1)
    mx_dequant_transpose(g["p1d"], g["dq1d_s"], g["dq1d_g"], 256, 8192, 0)

    C.barrier()


def emit_phase_A(C, nc, env, layer):
    g = env
    s1, s2, s3, sp = g["s1"], g["s2"], g["s3"], g["sp"]
    PS = g["PS"]
    ident = g["ident"]
    x_src = g["x_in"] if layer == 0 else g["x0d"]
    xn_fm = s1(0, 16384).rearrange("p (c t) -> p c t", t=1024)
    dqx_fm = s2(0, 16384).rearrange("p (c t) -> p c t", t=1024) if layer == 1 else None

    pre = "a"
    xt_done = []
    tile_done = []
    for t in range(NT):
        sl = t % 2
        xt = s3(0, 2048) if sl == 0 else s3(2048, 4096)
        xn_tm = s3(4096, 6144) if sl == 0 else s3(6144, 8192)
        bns = s3(8192, 8216)
        mv = s3(8216, 8218)
        uvar = s3(8218, 8219)
        sd = s3(8219, 8220)
        rstd = s3(8220, 8221)
        nbias = s3(8221, 8222)
        lsem = "lda_%d" % sl
        if t >= 2:
            C.wait("sync", "s_" + pre, xt_done[t - 2])
        C.dma("sync", xt, x_src[t * 128:(t + 1) * 128, :], lsem)
        C.wait("vector", lsem, C.cnt[lsem])
        for j in range(4):
            C.op("vector", lambda e, bns=bns, xt=xt, j=j: e.bn_stats(
                bns[:, j * 6:(j + 1) * 6], xt[:, j * 512:(j + 1) * 512]),
                outs=[bns], ins=[xt])
        C.op("vector", lambda e, mv=mv, bns=bns: e.bn_aggr(mv, bns),
             outs=[mv], ins=[bns])
        C.op("vector", lambda e, uvar=uvar, mv=mv: e.tensor_scalar_add(
            uvar, mv[:, 1:2], LN_EPS), outs=[uvar], ins=[mv], inc="v_" + pre)
        C.wait("scalar", "v_" + pre, C.cnt["v_" + pre])
        C.op("scalar", lambda e, sd=sd, uvar=uvar: e.sqrt(out=sd, in_=uvar),
             outs=[sd], ins=[uvar], inc="s_" + pre)
        C.wait("vector", "s_" + pre, C.cnt["s_" + pre])
        C.op("vector", lambda e, rstd=rstd, sd=sd: e.reciprocal(rstd, sd),
             outs=[rstd], ins=[sd])
        C.op("vector", lambda e, nbias=nbias, mv=mv, rstd=rstd: e.scalar_tensor_tensor(
            out=nbias, in0=mv[:, 0:1], scalar=-1.0, in1=rstd,
            op0=ALU.mult, op1=ALU.mult), outs=[nbias], ins=[mv, rstd], inc="v_" + pre)
        C.wait("scalar", "v_" + pre, C.cnt["v_" + pre])
        if t >= 2:
            C.wait("scalar", "t_" + pre, tile_done[t - 2][0])
            if layer == 1:
                C.wait("scalar", "gxn%d_" % sl + pre, tile_done[t - 2][2])
        C.op("scalar", lambda e, xn_tm=xn_tm, xt=xt, rstd=rstd, nbias=nbias: e.activation(
            out=xn_tm, in_=xt, func=AF.Identity, scale=rstd, bias=nbias),
            outs=[xn_tm], ins=[xt], inc="s_" + pre)
        xt_done.append(C.cnt["s_" + pre])
        C.wait("tensor", "s_" + pre, C.cnt["s_" + pre])
        for cc in range(16):
            pban = PS[cc % 2]
            C.wait("tensor", "ps%d_" % (cc % 2) + pre,
                   C.cnt.get("ps%d_" % (cc % 2) + pre, 0))
            C.op("tensor", lambda e, pban=pban, xn_tm=xn_tm, cc=cc: e.transpose(
                pban[:, 0:128], xn_tm[:, cc * 128:(cc + 1) * 128], ident),
                inc="t_" + pre)
            eng = "scalar" if cc % 2 == 0 else "vector"
            C.wait(eng, "t_" + pre, C.cnt["t_" + pre])
            if eng == "scalar":
                C.op(eng, lambda e, pban=pban, cc=cc, t=t: e.copy(
                    out=xn_fm[:, cc, t * 128:(t + 1) * 128], in_=pban[:, 0:128]),
                    outs=[xn_fm], inc="ps%d_" % (cc % 2) + pre)
            else:
                C.op(eng, lambda e, pban=pban, cc=cc, t=t: e.tensor_copy(
                    xn_fm[:, cc, t * 128:(t + 1) * 128], pban[:, 0:128]),
                    outs=[xn_fm], inc="ps%d_" % (cc % 2) + pre)
        if layer == 1:
            C.wait("gpsimd", "s_" + pre, C.cnt["s_" + pre])
            C.dma("gpsimd", g["xnd"][t * 128:(t + 1) * 128, :], xn_tm,
                  "gxn%d_" % sl + pre)
        tile_done.append((C.cnt["t_" + pre], C.cnt.get("v_" + pre, 0),
                          C.cnt.get("gxn%d_" % sl + pre, 0)))
    C.barrier()


def emit_phase_B(C, nc, env, layer):
    g = env
    s1, s2, s3, sp = g["s1"], g["s2"], g["s3"], g["sp"]
    PS = g["PS"]
    ident = g["ident"]
    ones2d = g["ones2d"]
    ropec, ropes = g["ropec"], g["ropes"]
    xn_fm = s1(0, 16384).rearrange("p (c t) -> p c t", t=1024)
    pre = "b"

    if layer == 0:
        plan = [("q", g["w0q_t"], g["b0q"], 0, 4), ("k", g["w0k_t"], g["b0k"], 0, 1),
                ("v", g["w0v_t"], g["b0v"], 0, 1)]
        wsteps = 2
    else:
        plan = [("q", g["w1q_t"], g["b1q"], 0, 4), ("k", g["w1k_t"], g["b1k"], 0, 4),
                ("v", g["w1v_t"], g["b1v"], 0, 4)]

    brow = {}
    off = 3200
    for kind, wt, bi, _, nch in plan:
        width = nch * 512
        brow[kind] = s3(off, off + width)[0:1, :]
        C.dma("sync", brow[kind], bi[:, :], "ldpro_" + pre)
        off += width

    wstage = [s2(0, 8192).rearrange("p (c o) -> p c o", o=512),
              s2(8192, 16384).rearrange("p (c o) -> p c o", o=512)]
    ochunks = []
    for kind, wt, bi, _, nch in plan:
        for oc in range(nch):
            ochunks.append((kind, wt, oc))

    gaincol = s3(2688, 2704)
    if layer == 0:
        C.dma("sync", gaincol, g["gain0"], "ldpro_" + pre)
        C.wait("vector", "ldpro_" + pre, C.cnt["ldpro_" + pre])
        C.op("vector", lambda e: e.tensor_scalar_mul(gaincol, gaincol, SCALE_HD),
             outs=[gaincol], ins=[gaincol], inc="v_" + pre)

    def load_w(i):
        kind, wt, oc = ochunks[i]
        dst = wstage[i % 2]
        src = wt.rearrange("(c p) o -> p c o", p=128)[:, :, oc * 512:(oc + 1) * 512]
        C.dma("sync", dst, src, "lw%d_" % (i % 2) + pre)

    nO = len(ochunks)
    mm_done = []
    st = dict(ps_done={0: 0, 1: 0})
    env["_Bst"] = st
    for i in range(min(2, nO)):
        load_w(i)

    C.wait("tensor", "ldpro_" + pre, C.cnt["ldpro_" + pre])
    for i, (kind, wt, oc) in enumerate(ochunks):
        wbuf = wstage[i % 2]
        C.wait("tensor", "lw%d_" % (i % 2) + pre, C.cnt["lw%d_" % (i % 2) + pre])
        for t in range(NT):
            bank = t % 2
            ps = PS[bank]
            C.wait("tensor", "s_" + pre, st["ps_done"][bank])
            for dc in range(16):
                C.op("tensor", lambda e, ps=ps, dc=dc, t=t, wbuf=wbuf: e.matmul(
                    ps[:, :], xn_fm[:, dc, t * 128:(t + 1) * 128], wbuf[:, dc, :],
                    start=(dc == 0), stop=False))
            C.op("tensor", lambda e, ps=ps, kind=kind, oc=oc: e.matmul(
                ps[:, :], ones2d[0:1, :], brow[kind][:, oc * 512:(oc + 1) * 512],
                start=False, stop=True), inc="t_" + pre)
            emit_B_evac(C, nc, env, layer, kind, oc, t, ps, pre, gaincol)
        mm_done.append(C.cnt["t_" + pre])
        if i + 2 < nO:
            C.wait("sync", "t_" + pre, mm_done[i])
            load_w(i + 2)
    C.barrier()


def emit_B_evac(C, nc, env, layer, kind, oc, t, ps, pre, gaincol):
    g = env
    s3 = g["s3"]
    PS = g["PS"]
    ident = g["ident"]
    ropec, ropes = g["ropec"], g["ropes"]
    st = env["_Bst"]
    st.setdefault("qkv_done", {0: None, 1: None})
    st.setdefault("qr_done", {0: 0, 1: 0})
    st.setdefault("tb_done", {0: 0, 1: 0})
    st.setdefault("ev_done", {0: 0, 1: 0})
    sl = t % 2
    bank = sl
    qkv = s3(0, 512) if sl == 0 else s3(512, 1024)
    qr = s3(1024, 1536) if sl == 0 else s3(1536, 2048)
    sq = s3(2048, 2560)
    msq4 = s3(2560, 2564)
    uu4 = s3(2564, 2568)
    sd4 = s3(2568, 2572)
    rs4 = s3(2572, 2576)
    evst = [s3(2832, 2960), s3(2960, 3088)]
    tmp1 = sq[:, 256:512].rearrange("p (h k) -> p h k", k=64)
    tmp2 = sq[:, 0:256].rearrange("p (h k) -> p h k", k=64)
    cosT = ropec[:, t, :].unsqueeze(1).broadcast_to([128, 4, 64])
    sinT = ropes[:, t, :].unsqueeze(1).broadcast_to([128, 4, 64])

    C.wait("scalar", "t_" + pre, C.cnt["t_" + pre])
    need_rope = (kind in ("q", "k"))
    if st["qkv_done"][sl] is not None:
        sem, val = st["qkv_done"][sl]
        C.wait("scalar", sem, val)
    C.op("scalar", lambda e, qkv=qkv, ps=ps: e.copy(out=qkv, in_=ps[:, :]),
         outs=[qkv], inc="s_" + pre)
    st["ps_done"][bank] = C.cnt["s_" + pre]

    if not need_rope:
        C.wait("gpsimd", "s_" + pre, C.cnt["s_" + pre])
        vdst = g["v0s"] if layer == 0 else g["v1s"]
        C.dma("gpsimd", vdst[t * 128:(t + 1) * 128, oc * 512:(oc + 1) * 512],
              qkv, "gv%d_" % sl + pre)
        st["qkv_done"][sl] = ("gv%d_" % sl + pre, C.cnt["gv%d_" % sl + pre])
        return

    qv = qkv.rearrange("p (h d) -> p h d", d=128)
    qrv = qr.rearrange("p (h d) -> p h d", d=128)
    C.wait("scalar", "v_" + pre, st.get("sq_free", 0))
    if layer == 0:
        C.op("scalar", lambda e: e.activation(out=sq, in_=qkv, func=AF.Square),
             outs=[sq], ins=[qkv], inc="s_" + pre)
        C.wait("vector", "s_" + pre, C.cnt["s_" + pre])
        C.wait("vector", "t2_" + pre, st["qr_done"][sl])
        C.op("vector", lambda e: e.tensor_reduce(
            msq4, sq.rearrange("p (h d) -> p h d", d=128), axis=AX.X,
            op=ALU.add), outs=[msq4], ins=[sq])
        C.op("vector", lambda e: e.tensor_scalar(
            uu4, msq4, 1.0 / 128.0, RMS_EPS, ALU.mult, ALU.add),
            outs=[uu4], ins=[msq4], inc="v_" + pre)
        C.wait("scalar", "v_" + pre, C.cnt["v_" + pre])
        C.op("scalar", lambda e: e.sqrt(out=sd4, in_=uu4),
             outs=[sd4], ins=[uu4], inc="s_" + pre)
        C.wait("vector", "s_" + pre, C.cnt["s_" + pre])
        C.op("vector", lambda e: e.reciprocal(rs4, sd4),
             outs=[rs4], ins=[sd4])
        if kind == "q":
            C.op("vector", lambda e, oc=oc: e.tensor_tensor(
                rs4, rs4, gaincol[:, oc * 4:(oc + 1) * 4], ALU.mult),
                outs=[rs4], ins=[rs4, gaincol])
        C.op("vector", lambda e: e.tensor_tensor(
            qv, qv, rs4.unsqueeze(2).broadcast_to([128, 4, 128]), ALU.mult),
            outs=[qkv], ins=[qkv, rs4])
        q1 = qv[:, :, 0:64]; q2 = qv[:, :, 64:128]
        r1 = qrv[:, :, 0:64]; r2 = qrv[:, :, 64:128]
        C.op("vector", lambda e: e.tensor_tensor(tmp1, q1, cosT, ALU.mult),
             outs=[tmp1], ins=[qkv])
        C.op("vector", lambda e: e.tensor_tensor(tmp2, q2, sinT, ALU.mult),
             outs=[sq], ins=[qkv])
        C.op("vector", lambda e: e.tensor_tensor(r1, tmp1, tmp2, ALU.add),
             outs=[qr], ins=[tmp1, sq])
        C.op("vector", lambda e: e.tensor_tensor(tmp1, q1, sinT, ALU.mult),
             outs=[tmp1], ins=[qkv])
        C.op("vector", lambda e: e.tensor_tensor(tmp2, q2, cosT, ALU.mult),
             outs=[sq], ins=[qkv])
        C.op("vector", lambda e: e.tensor_tensor(r2, tmp2, tmp1, ALU.subtract),
             outs=[qr], ins=[tmp1, sq], inc="v_" + pre)
    else:
        C.wait("vector", "s_" + pre, C.cnt["s_" + pre])
        C.wait("vector", "t2_" + pre, st["qr_done"][sl])
        qp = qkv.rearrange("p (h d two) -> p h d two", d=64, two=2)
        rp = qr.rearrange("p (h d two) -> p h d two", d=64, two=2)
        qe = qp[:, :, :, 0]; qo = qp[:, :, :, 1]
        re_ = rp[:, :, :, 0]; ro = rp[:, :, :, 1]
        C.op("vector", lambda e: e.tensor_tensor(tmp1, qe, cosT, ALU.mult),
             outs=[tmp1], ins=[qkv])
        C.op("vector", lambda e: e.tensor_tensor(tmp2, qo, sinT, ALU.mult),
             outs=[sq], ins=[qkv])
        C.op("vector", lambda e: e.tensor_tensor(re_, tmp1, tmp2, ALU.subtract),
             outs=[qr], ins=[tmp1, sq])
        C.op("vector", lambda e: e.tensor_tensor(tmp1, qe, sinT, ALU.mult),
             outs=[tmp1], ins=[qkv])
        C.op("vector", lambda e: e.tensor_tensor(tmp2, qo, cosT, ALU.mult),
             outs=[sq], ins=[qkv])
        if kind == "q":
            C.op("vector", lambda e: e.tensor_tensor(ro, tmp1, tmp2, ALU.add),
                 outs=[qr], ins=[tmp1, sq])
            C.op("vector", lambda e: e.tensor_scalar_mul(qr, qr, SCALE_HD),
                 outs=[qr], ins=[qr], inc="v_" + pre)
        else:
            C.op("vector", lambda e: e.tensor_tensor(ro, tmp1, tmp2, ALU.add),
                 outs=[qr], ins=[tmp1, sq], inc="v_" + pre)
    st["qkv_done"][sl] = ("v_" + pre, C.cnt["v_" + pre])
    st["sq_free"] = C.cnt["v_" + pre]

    C.wait("tensor", "v_" + pre, C.cnt["v_" + pre])
    for hs in range(4):
        h = oc * 4 + hs
        tb = hs % 2
        pban = PS[2 + tb]
        evsem = "sv2_%d_" % tb + pre
        C.wait("tensor", evsem, st["tb_done"][tb])
        C.op("tensor", lambda e, pban=pban, hs=hs: e.transpose(
            pban[:, 0:128], qr[:, hs * 128:(hs + 1) * 128], ident),
            inc="t2_" + pre)
        eng = "scalar" if tb == 0 else "vector"
        C.wait(eng, "t2_" + pre, C.cnt["t2_" + pre])
        C.wait(eng, "gst%d_" % tb + pre, st["ev_done"][tb])
        ev = evst[tb]
        if eng == "scalar":
            C.op(eng, lambda e, ev=ev, pban=pban: e.copy(out=ev, in_=pban[:, 0:128]),
                 outs=[ev], inc=evsem)
        else:
            C.op(eng, lambda e, ev=ev, pban=pban: e.tensor_copy(ev, pban[:, 0:128]),
                 outs=[ev], inc=evsem)
        st["tb_done"][tb] = C.cnt[evsem]
        C.wait("gpsimd", evsem, C.cnt[evsem])
        if kind == "q":
            C.dma("gpsimd", g["qd"][h, :, t * 128:(t + 1) * 128], ev,
                  "gst%d_" % tb + pre)
        else:
            kdst = g["k0s"] if layer == 0 else g["k1s"]
            C.dma("gpsimd", kdst[h * 128:(h + 1) * 128, t * 128:(t + 1) * 128], ev,
                  "gst%d_" % tb + pre)
        st["ev_done"][tb] = C.cnt["gst%d_" % tb + pre]
    st["qr_done"][sl] = C.cnt["t2_" + pre]


def emit_gather(C, nc, env, layer):
    g = env
    C.barrier(engs=("gpsimd",))
    ks, kg = (g["k0s"], g["k0g"]) if layer == 0 else (g["k1s"], g["k1g"])
    vs, vg = (g["v0s"], g["v0g"]) if layer == 0 else (g["v1s"], g["v1g"])
    C.op("gpsimd", lambda e: e.collective_compute(
        "AllGather", ALU.bypass, replica_groups=RG,
        ins=[ks.opt()], outs=[kg.opt()]), inc="cc")
    C.op("gpsimd", lambda e: e.collective_compute(
        "AllGather", ALU.bypass, replica_groups=RG,
        ins=[vs.opt()], outs=[vg.opt()]), inc="cc")
    C.cnt["cc_g%d" % layer] = C.cnt["cc"]


def emit_ffn0(C, nc, env):
    g = env
    s1, s2, s3, sp = g["s1"], g["s2"], g["s3"], g["sp"]
    PS = g["PS"]
    pre = "c"
    xn_fm = s1(0, 16384).rearrange("p (c t) -> p c t", t=1024)
    hi = s2(0, 8192).bitcast(BF16).rearrange("p (c t) -> p c t", t=1024)
    lo = s2(8192, 16384).bitcast(BF16).rearrange("p (c t) -> p c t", t=1024)
    for cc in range(16):
        C.op("scalar", lambda e, cc=cc: e.copy(out=hi[:, cc, :], in_=xn_fm[:, cc, :]),
             outs=[hi], ins=[xn_fm], inc="s_" + pre)
        C.wait("vector", "s_" + pre, C.cnt["s_" + pre])
        C.op("vector", lambda e, cc=cc: e.tensor_tensor(
            lo[:, cc, :], xn_fm[:, cc, :], hi[:, cc, :], ALU.subtract),
            outs=[lo], ins=[xn_fm, hi], inc="v_" + pre)
    C.barrier()

    fh = s1(0, 6144).bitcast(BF16).rearrange("p (c t) -> p c t", t=256)
    fl = s1(6144, 12288).bitcast(BF16).rearrange("p (c t) -> p c t", t=256)
    gw = [s1(12288, 12800).bitcast(FP8).rearrange("p (c i) -> p c i", i=128),
          s1(12800, 13312).bitcast(FP8).rearrange("p (c i) -> p c i", i=128)]
    uw = [s1(13312, 13824).bitcast(FP8).rearrange("p (c i) -> p c i", i=128),
          s1(13824, 14336).bitcast(FP8).rearrange("p (c i) -> p c i", i=128)]
    gsc = s1(14336, 14384); usc = s1(14384, 14432)
    gate_t = s1(14432, 14944)[:, 0:256]
    fuse32 = s1(14944, 15456)[:, 0:256]
    dwst = [s3(0, 3072).bitcast(FP8).rearrange("p (c e) -> p c e", e=256),
            s3(3072, 6144).bitcast(FP8).rearrange("p (c e) -> p c e", e=256)]
    dsbc = s3(6144, 8192)
    gout = [s3(8192, 8704), s3(8704, 9216)]
    gtmp = [s3(9216, 9728), s3(9728, 10240)]

    C.dma("sync", gsc, g["gs0"], "ldpro_" + pre)
    C.dma("sync", usc, g["us0"], "ldpro_" + pre)
    C.dma("sync", dsbc, g["ds0bc"], "ldpro_" + pre)

    wq8g, wq8u, wq8d = g["q0g_g"], g["q0u_g"], g["q0d_g"]
    C.wait("sync", "cc", 3)
    lw_done = {}
    lw2_done = {}
    fuse_done = {}
    down_read = 0
    ps_done = {0: 0, 1: 0}
    psd_done = {2: 0, 3: 0}
    gout_done = {0: 0, 1: 0}
    gtmp_done = {0: 0, 1: 0}

    def load_gu(tb, ic, buf):
        src_g = wq8g.rearrange("(c p) i -> p c i", p=128)[:, :, ic * 128:(ic + 1) * 128]
        src_u = wq8u.rearrange("(c p) i -> p c i", p=128)[:, :, ic * 128:(ic + 1) * 128]
        C.wait("sync", "t_" + pre, lw_done.get((tb, ic - 2), 0))
        C.dma("sync", gw[buf], src_g, "lwg%d_" % buf + pre)
        C.dma("sync", uw[buf], src_u, "lwg%d_" % buf + pre)

    def load_dw(tb, ec, buf):
        src = wq8d.rearrange("(c p) e -> p c e", p=128)[:, :, ec * 256:(ec + 1) * 256]
        C.wait("sync", "t2_" + pre, lw2_done.get((tb, ec - 2), 0))
        C.dma("sync", dwst[buf], src, "lwd%d_" % buf + pre)

    C.wait("scalar", "ldpro_" + pre, C.cnt["ldpro_" + pre])
    C.wait("vector", "ldpro_" + pre, C.cnt["ldpro_" + pre])
    for tb in range(4):
        tsl = slice(tb * 256, (tb + 1) * 256)
        load_gu(tb, 0, 0)
        load_gu(tb, 1, 1)
        for ic in range(48):
            buf = ic % 2
            C.wait("tensor", "lwg%d_" % buf + pre, C.cnt["lwg%d_" % buf + pre])
            psg, psu = PS[0], PS[1]
            C.wait("tensor", "s_" + pre, ps_done[0])
            C.wait("tensor", "v_" + pre, ps_done[1])
            if tb > 0 and ic == 0:
                C.wait("tensor", "t2_" + pre, down_read)
            for hc in range(16):
                C.op("tensor", lambda e, hc=hc, buf=buf, psg=psg, tsl=tsl: e.matmul(
                    psg[:, 0:256], gw[buf][:, hc, :], hi[:, hc, tsl],
                    start=(hc == 0), stop=False))
                C.op("tensor", lambda e, hc=hc, buf=buf, psg=psg, tsl=tsl: e.matmul(
                    psg[:, 0:256], gw[buf][:, hc, :], lo[:, hc, tsl],
                    start=False, stop=(hc == 15)))
            for hc in range(16):
                C.op("tensor", lambda e, hc=hc, buf=buf, psu=psu, tsl=tsl: e.matmul(
                    psu[:, 0:256], uw[buf][:, hc, :], hi[:, hc, tsl],
                    start=(hc == 0), stop=False))
                C.op("tensor", lambda e, hc=hc, buf=buf, psu=psu, tsl=tsl: e.matmul(
                    psu[:, 0:256], uw[buf][:, hc, :], lo[:, hc, tsl],
                    start=False, stop=(hc == 15)),
                    inc=("t_" + pre) if hc == 15 else None)
            lw_done[(tb, ic)] = C.cnt["t_" + pre]
            if ic + 2 < 48:
                load_gu(tb, ic + 2, buf)
            C.wait("scalar", "t_" + pre, C.cnt["t_" + pre])
            C.op("scalar", lambda e, psg=psg, ic=ic: e.activation(
                out=gate_t, in_=psg[:, 0:256], func=AF.Silu, scale=gsc[:, ic:ic + 1]),
                outs=[gate_t], ins=[gsc], inc="s_" + pre)
            ps_done[0] = C.cnt["s_" + pre]
            C.wait("vector", "s_" + pre, C.cnt["s_" + pre])
            C.op("vector", lambda e, psu=psu, ic=ic: e.scalar_tensor_tensor(
                out=fuse32, in0=psu[:, 0:256], scalar=usc[:, ic:ic + 1], in1=gate_t,
                op0=ALU.mult, op1=ALU.mult), outs=[fuse32], ins=[usc, gate_t],
                inc="v_" + pre)
            ps_done[1] = C.cnt["v_" + pre]
            C.wait("scalar", "v_" + pre, C.cnt["v_" + pre])
            C.op("scalar", lambda e, ic=ic: e.copy(out=fh[:, ic, :], in_=fuse32),
                 outs=[fh], ins=[fuse32], inc="s_" + pre)
            C.wait("vector", "s_" + pre, C.cnt["s_" + pre])
            C.op("vector", lambda e, ic=ic: e.tensor_tensor(
                fl[:, ic, :], fuse32, fh[:, ic, :], ALU.subtract),
                outs=[fl], ins=[fuse32, fh], inc="v_" + pre)
        fuse_done[tb] = C.cnt["v_" + pre]

        load_dw(tb, 0, 0)
        load_dw(tb, 1, 1)
        C.wait("tensor", "v_" + pre, fuse_done[tb])
        for ec in range(8):
            buf = ec % 2
            C.wait("tensor", "lwd%d_" % buf + pre, C.cnt["lwd%d_" % buf + pre])
            for ts_ in range(2):
                t_abs = tb * 2 + ts_
                bank = 2 + ts_
                psd = PS[bank]
                C.wait("tensor", "v2_" + pre, psd_done[bank])
                for ic in range(48):
                    C.op("tensor", lambda e, ic=ic, ts_=ts_, psd=psd, buf=buf: e.matmul(
                        psd[:, 0:256], fh[:, ic, ts_ * 128:(ts_ + 1) * 128],
                        dwst[buf][:, ic, :], start=(ic == 0), stop=False))
                    C.op("tensor", lambda e, ic=ic, ts_=ts_, psd=psd, buf=buf: e.matmul(
                        psd[:, 0:256], fl[:, ic, ts_ * 128:(ts_ + 1) * 128],
                        dwst[buf][:, ic, :], start=False, stop=(ic == 47)),
                        inc=("t2_" + pre) if ic == 47 else None)
                C.wait("vector", "t2_" + pre, C.cnt["t2_" + pre])
                C.wait("vector", "s2_" + pre, gtmp_done[ts_])
                C.op("vector", lambda e, psd=psd, ec=ec, ts_=ts_: e.tensor_tensor(
                    gtmp[ts_][:, 0:256], psd[:, 0:256],
                    dsbc[:, ec * 256:(ec + 1) * 256], ALU.mult),
                    outs=[gtmp[ts_]], ins=[dsbc], inc="v2_" + pre)
                psd_done[bank] = C.cnt["v2_" + pre]
                C.wait("scalar", "v2_" + pre, C.cnt["v2_" + pre])
                C.wait("scalar", "gst%d_" % ts_ + pre, gout_done[ts_])
                C.op("scalar", lambda e, ts_=ts_: e.activation(
                    out=gout[ts_][:, 0:256], in_=gtmp[ts_][:, 0:256], func=AF.Silu),
                    outs=[gout[ts_]], ins=[gtmp[ts_]], inc="s2_" + pre)
                gtmp_done[ts_] = C.cnt["s2_" + pre]
                C.wait("gpsimd", "s2_" + pre, C.cnt["s2_" + pre])
                C.dma("gpsimd", g["gd"][t_abs, :, ec * 256:(ec + 1) * 256],
                      gout[ts_][:, 0:256], "gst%d_" % ts_ + pre)
                gout_done[ts_] = C.cnt["gst%d_" % ts_ + pre]
            lw2_done[(tb, ec)] = C.cnt["t2_" + pre]
            if ec + 2 < 8:
                load_dw(tb, ec + 2, buf)
        down_read = C.cnt["t2_" + pre]
    C.barrier()


def emit_ffn1(C, nc, env):
    g = env
    s1, s2, s3, sp = g["s1"], g["s2"], g["s3"], g["sp"]
    PS = g["PS"]
    ident = g["ident"]
    pre = "c"
    ident = g["ident"]
    dqx_fm = s1(0, 16384).rearrange("p (c t) -> p c t", t=1024)
    PS = g["PS"]
    xq_done = {0: 0, 1: 0}
    pb_done = {0: 0, 1: 0}
    tq_done = {}
    for t in range(NT):
        sl = t % 2
        xnt = s3(0, 2048) if sl == 0 else s3(2048, 4096)
        dqt = s3(4096, 6144) if sl == 0 else s3(6144, 8192)
        amx0 = s3(8192, 8256); scl0 = s3(8256, 8320); rcp0 = s3(8320, 8384)
        rc20 = s3(8384, 8448); sc20 = s3(8448, 8512)
        q8t0 = s3(8512, 9024).bitcast(FP8)[:, 0:2048]
        C.wait("sync", "vq%d_" % sl + pre, xq_done[sl])
        C.dma("sync", xnt, g["xnd"][t * 128:(t + 1) * 128, :], "lxq%d_" % sl + pre)
        C.wait("vector", "lxq%d_" % sl + pre, C.cnt["lxq%d_" % sl + pre])
        xnv = xnt.rearrange("p (b k) -> p b k", k=32)
        C.op("vector", lambda e, amx0=amx0, xnv=xnv: e.reduce_max(
            amx0, xnv, axis=AX.X, apply_absolute_value=True),
            outs=[amx0], ins=[xnt])
        C.op("vector", lambda e, scl0=scl0, amx0=amx0: e.tensor_scalar(
            scl0, amx0, 1.0 / 448.0, 1e-30, ALU.mult, ALU.max),
            outs=[scl0], ins=[amx0])
        C.op("vector", lambda e, rcp0=rcp0, scl0=scl0: e.reciprocal(rcp0, scl0),
             outs=[rcp0], ins=[scl0])
        C.op("vector", lambda e, rc20=rc20, rcp0=rcp0: e.tensor_scalar_mul(rc20, rcp0, 0.5),
             outs=[rc20], ins=[rcp0])
        C.op("vector", lambda e, sc20=sc20, scl0=scl0: e.tensor_scalar_mul(sc20, scl0, 2.0),
             outs=[sc20], ins=[scl0])
        q8v0 = q8t0.rearrange("p (b k) -> p b k", k=32)
        C.op("vector", lambda e, q8v0=q8v0, xnv=xnv, rc20=rc20: e.tensor_tensor(
            q8v0, xnv, rc20.unsqueeze(2).broadcast_to([128, 64, 32]), ALU.mult),
            outs=[q8t0], ins=[xnt, rc20])
        dqv0 = dqt.rearrange("p (b k) -> p b k", k=32)
        C.wait("vector", "tq_" + pre, tq_done.get(sl, 0))
        C.op("vector", lambda e, dqv0=dqv0, q8v0=q8v0, sc20=sc20: e.tensor_tensor(
            dqv0, q8v0, sc20.unsqueeze(2).broadcast_to([128, 64, 32]), ALU.mult),
            outs=[dqt], ins=[q8t0, sc20], inc="vq%d_" % sl + pre)
        xq_done[sl] = C.cnt["vq%d_" % sl + pre]
        C.wait("tensor", "vq%d_" % sl + pre, C.cnt["vq%d_" % sl + pre])
        for cc in range(16):
            bb = cc % 2
            pban = PS[bb]
            C.wait("tensor", "pq%d_" % bb + pre, pb_done[bb])
            C.op("tensor", lambda e, pban=pban, dqt=dqt, cc=cc: e.transpose(
                pban[:, 0:128], dqt[:, cc * 128:(cc + 1) * 128], ident),
                inc="tq_" + pre)
            eng = "scalar" if bb == 0 else "vector"
            C.wait(eng, "tq_" + pre, C.cnt["tq_" + pre])
            if eng == "scalar":
                C.op(eng, lambda e, pban=pban, cc=cc, t=t: e.copy(
                    out=dqx_fm[:, cc, t * 128:(t + 1) * 128], in_=pban[:, 0:128]),
                    outs=[dqx_fm], inc="pq%d_" % bb + pre)
            else:
                C.op(eng, lambda e, pban=pban, cc=cc, t=t: e.tensor_copy(
                    dqx_fm[:, cc, t * 128:(t + 1) * 128], pban[:, 0:128]),
                    outs=[dqx_fm], inc="pq%d_" % bb + pre)
            pb_done[bb] = C.cnt["pq%d_" % bb + pre]
        tq_done[sl] = C.cnt["tq_" + pre]
    C.barrier()

    gw = [s2(0, 4096).rearrange("p (c o) -> p c o", o=256),
          s2(4096, 8192).rearrange("p (c o) -> p c o", o=256)]
    uw = [s2(8192, 12288).rearrange("p (c o) -> p c o", o=256),
          s2(12288, 16384).rearrange("p (c o) -> p c o", o=256)]
    dw = [s3(0, 4096).rearrange("p (c e) -> p c e", e=2048),
          s3(4096, 8192).rearrange("p (c e) -> p c e", e=2048)]
    dpart = s3(8192, 12288).rearrange("p (u e) -> p u e", e=2048)
    fuse_t = sp(1024, 1280)
    gate_t = sp(1280, 1536)
    dqf_t = sp(1536, 1792)
    amx = sp(264, 272); scl = sp(272, 280); rcp = sp(280, 288)
    rc2 = sp(288, 296); sc2 = sp(296, 304)
    q8t = sp(304, 368).bitcast(FP8)[:, 0:256]
    dqf_fm = sp(1792, 2048).rearrange("p (c t) -> p c t", t=128)
    gout = [sp(368, 624), sp(624, 880)]

    dq1g, dq1u, dq1d = g["dq1g_g"], g["dq1u_g"], g["dq1d_g"]
    C.wait("sync", "cc", 6)
    gu_done = {0: 0, 1: 0}
    dw_done = {0: 0, 1: 0}
    psgu_done = {0: 0, 1: 0}
    tb_done = {2: 0, 3: 0}
    psd_done = {4: 0, 5: 0}
    gout_done = {0: 0, 1: 0}
    dqf_read = 0
    dpart_done = 0

    def load_gu(io, buf):
        rank = (io * 256) // 1024
        iloc = (io * 256) % 1024
        src_g = dq1g[rank * 2048:(rank + 1) * 2048, iloc:iloc + 256]
        src_u = dq1u[rank * 2048:(rank + 1) * 2048, iloc:iloc + 256]
        C.wait("sync", "t_" + pre, gu_done[buf])
        C.dma("sync", gw[buf], src_g.rearrange("(c p) i -> p c i", p=128),
              "lwg%d_" % buf + pre)
        C.dma("sync", uw[buf], src_u.rearrange("(c p) i -> p c i", p=128),
              "lwg%d_" % buf + pre)

    def load_dw(io, buf):
        C.wait("sync", "t3_" + pre, dw_done[buf])
        for eb in range(8):
            src = dq1d[eb * 8192 + io * 256: eb * 8192 + (io + 1) * 256, :]
            C.dma("sync", dw[buf][:, :, eb * 256:(eb + 1) * 256],
                  src.rearrange("(c p) e -> p c e", p=128), "lwd%d_" % buf + pre)

    C.barrier()
    for tg in range(4):
        C.wait("vector", "s2_" + pre, dpart_done)
        C.op("vector", lambda e: e.memset(dpart[:, :, :], 0.0), outs=[dpart],
             inc="v0_" + pre)
        load_gu(0, 0)
        load_dw(0, 0)
        load_gu(1, 1)
        load_dw(1, 1)
        for io in range(32):
            buf = io % 2
            C.wait("tensor", "lwg%d_" % buf + pre, C.cnt["lwg%d_" % buf + pre])
            C.wait("tensor", "lwd%d_" % buf + pre, C.cnt["lwd%d_" % buf + pre])
            for u in range(2):
                t = tg * 2 + u
                psg, psu = PS[0], PS[1]
                C.wait("tensor", "s_" + pre, psgu_done[0])
                C.wait("tensor", "v_" + pre, psgu_done[1])
                for hc in range(16):
                    C.op("tensor", lambda e, hc=hc, t=t, buf=buf, psg=psg: e.matmul(
                        psg[:, 0:256], dqx_fm[:, hc, t * 128:(t + 1) * 128],
                        gw[buf][:, hc, :], start=(hc == 0), stop=(hc == 15)))
                for hc in range(16):
                    C.op("tensor", lambda e, hc=hc, t=t, buf=buf, psu=psu: e.matmul(
                        psu[:, 0:256], dqx_fm[:, hc, t * 128:(t + 1) * 128],
                        uw[buf][:, hc, :], start=(hc == 0), stop=(hc == 15)),
                        inc=("t_" + pre) if hc == 15 else None)
                C.wait("scalar", "t_" + pre, C.cnt["t_" + pre])
                C.op("scalar", lambda e, psg=psg: e.activation(
                    out=gate_t, in_=psg[:, 0:256], func=AF.Silu),
                    outs=[gate_t], inc="s_" + pre)
                psgu_done[0] = C.cnt["s_" + pre]
                C.wait("vector", "s_" + pre, C.cnt["s_" + pre])
                C.op("vector", lambda e, psu=psu: e.tensor_tensor(
                    fuse_t, gate_t, psu[:, 0:256], ALU.mult),
                    outs=[fuse_t], ins=[gate_t], inc="v_" + pre)
                psgu_done[1] = C.cnt["v_" + pre]
                fv = fuse_t.rearrange("p (b k) -> p b k", k=32)
                C.op("vector", lambda e, fv=fv: e.reduce_max(
                    amx, fv, axis=AX.X, apply_absolute_value=True),
                    outs=[amx], ins=[fuse_t])
                C.op("vector", lambda e: e.tensor_scalar(
                    scl, amx, 1.0 / 448.0, 1e-30, ALU.mult, ALU.max),
                    outs=[scl], ins=[amx])
                C.op("vector", lambda e: e.reciprocal(rcp, scl),
                     outs=[rcp], ins=[scl])
                C.op("vector", lambda e: e.tensor_scalar_mul(rc2, rcp, 0.5),
                     outs=[rc2], ins=[rcp])
                C.op("vector", lambda e: e.tensor_scalar_mul(sc2, scl, 2.0),
                     outs=[sc2], ins=[scl])
                q8v = q8t.rearrange("p (b k) -> p b k", k=32)
                C.op("vector", lambda e, q8v=q8v, fv=fv: e.tensor_tensor(
                    q8v, fv, rc2.unsqueeze(2).broadcast_to([128, 8, 32]), ALU.mult),
                    outs=[q8t], ins=[fuse_t, rc2])
                dqv = dqf_t.rearrange("p (b k) -> p b k", k=32)
                C.op("vector", lambda e, dqv=dqv, q8v=q8v: e.tensor_tensor(
                    dqv, q8v, sc2.unsqueeze(2).broadcast_to([128, 8, 32]), ALU.mult),
                    outs=[dqf_t], ins=[q8t, sc2], inc="v_" + pre)
                C.wait("tensor", "v_" + pre, C.cnt["v_" + pre])
                C.wait("tensor", "t3_" + pre, dqf_read)
                for cc in range(2):
                    bb = 2 + cc
                    pban = PS[bb]
                    C.wait("tensor", "sv%d_" % bb + pre, tb_done[bb])
                    C.op("tensor", lambda e, pban=pban, cc=cc: e.transpose(
                        pban[:, 0:128], dqf_t[:, cc * 128:(cc + 1) * 128], ident),
                        inc="t2_" + pre)
                    eng = "scalar" if cc == 0 else "vector"
                    C.wait(eng, "t2_" + pre, C.cnt["t2_" + pre])
                    if eng == "scalar":
                        C.op(eng, lambda e, pban=pban, cc=cc: e.copy(
                            out=dqf_fm[:, cc, :], in_=pban[:, 0:128]),
                            outs=[dqf_fm], inc="sv%d_" % bb + pre)
                    else:
                        C.op(eng, lambda e, pban=pban, cc=cc: e.tensor_copy(
                            dqf_fm[:, cc, :], pban[:, 0:128]),
                            outs=[dqf_fm], inc="sv%d_" % bb + pre)
                    tb_done[bb] = C.cnt["sv%d_" % bb + pre]
                C.wait("tensor", "sv2_" + pre, tb_done[2])
                C.wait("tensor", "sv3_" + pre, tb_done[3])
                for eb in range(4):
                    bank = 4 + eb % 2
                    psd = PS[bank]
                    C.wait("tensor", "v2_" + pre, psd_done[bank])
                    for cc in range(2):
                        C.op("tensor", lambda e, psd=psd, cc=cc, eb=eb, buf=buf: e.matmul(
                            psd[:, :], dqf_fm[:, cc, :],
                            dw[buf][:, cc, eb * 512:(eb + 1) * 512],
                            start=(cc == 0), stop=(cc == 1)),
                            inc=("t3_" + pre) if cc == 1 else None)
                    C.wait("vector", "t3_" + pre, C.cnt["t3_" + pre])
                    C.op("vector", lambda e, psd=psd, u=u, eb=eb: e.tensor_tensor(
                        dpart[:, u, eb * 512:(eb + 1) * 512],
                        dpart[:, u, eb * 512:(eb + 1) * 512], psd[:, :], ALU.add),
                        outs=[dpart], ins=[dpart], inc="v2_" + pre)
                    psd_done[bank] = C.cnt["v2_" + pre]
                dqf_read = C.cnt["t3_" + pre]
            gu_done[buf] = C.cnt["t_" + pre]
            dw_done[buf] = C.cnt["t3_" + pre]
            if io + 2 < 32:
                load_gu(io + 2, buf)
                load_dw(io + 2, buf)
        for u in range(2):
            t = tg * 2 + u
            for eb in range(8):
                sl = eb % 2
                C.wait("scalar", "v2_" + pre, C.cnt["v2_" + pre])
                C.wait("scalar", "gst%d_" % sl + pre, gout_done[sl])
                C.op("scalar", lambda e, u=u, eb=eb, sl=sl: e.activation(
                    out=gout[sl],
                    in_=dpart[:, u, eb * 256:(eb + 1) * 256], func=AF.Silu),
                    outs=[gout[sl]], ins=[dpart], inc="s2_" + pre)
                C.wait("gpsimd", "s2_" + pre, C.cnt["s2_" + pre])
                C.dma("gpsimd", g["gd"][t, :, eb * 256:(eb + 1) * 256], gout[sl],
                      "gst%d_" % sl + pre)
                gout_done[sl] = C.cnt["gst%d_" % sl + pre]
        dpart_done = C.cnt["s2_" + pre]
    C.barrier()


def emit_attn(C, nc, env, layer):
    g = env
    s1, s2, s3, sp = g["s1"], g["s2"], g["s3"], g["sp"]
    PS = g["PS"]
    ones2d = g["ones2d"]
    pre = "d"
    kg = g["k0g"] if layer == 0 else g["k1g"]
    vg = g["v0g"] if layer == 0 else g["v1g"]
    KVH = 4 if layer == 0 else 16
    C.wait("sync", "cc", C.cnt["cc_g%d" % layer])
    ao_fm = s1(0, 16384).rearrange("p (h t) -> p h t", t=1024)

    masks = s2(0, 4096).bitcast(BF16).rearrange("p (m q) -> p m q", q=512)
    qrow = sp(264, 264 + 1024 // 1)
    qrow = s3(12288 - 1024, 12288)[0:1, :]
    C.dma("sync", qrow, g["qpos_row"], "ld_" + pre)
    qbc = s2(4096, 4608)
    kcol = s2(4608, 4609)
    kcoli = s2(4609, 4610)
    m01 = s2(4610, 5122)
    C.wait("tensor", "ld_" + pre, C.cnt["ld_" + pre])
    C.op("tensor", lambda e: e.matmul(PS[6][:, :], ones2d[0:1, :], qrow[:, 0:512],
                                      start=True, stop=True), inc="t_" + pre)
    C.wait("vector", "t_" + pre, C.cnt["t_" + pre])
    C.op("vector", lambda e: e.tensor_copy(qbc, PS[6][:, :]), outs=[qbc])
    for rr in range(4):
        for kt in range(4):
            mi = rr * 4 + kt
            C.wait("gpsimd", "v_" + pre, C.cnt.get("v_" + pre, 0))
            C.op("gpsimd", lambda e, rr=rr, kt=kt: e.iota(
                kcoli.bitcast(I32), pattern=[[0, 1]], base=rr + 4 * (kt * 128),
                channel_multiplier=4), outs=[kcoli], inc="g_" + pre)
            C.wait("vector", "g_" + pre, C.cnt["g_" + pre])
            C.op("vector", lambda e: e.tensor_copy(kcol, kcoli.bitcast(I32)),
                 outs=[kcol], ins=[kcoli])
            C.op("vector", lambda e: e.tensor_scalar(
                m01, qbc, kcol, None, ALU.is_ge),
                outs=[m01], ins=[qbc, kcol])
            C.op("vector", lambda e, mi=mi: e.tensor_scalar(
                masks[:, mi, :], m01, 1.0, -NEG_BIG, ALU.subtract, ALU.mult),
                outs=[masks], ins=[m01], inc="v_" + pre)

    kbuf = [s2(5632, 9728), s2(9728, 13824)]
    vbuf = [s2(13824 + 0, 13824 + 0) if False else None, None]
    vbuf = [s3(0, 4096).rearrange("p (c h) -> p c h", h=128),
            s3(4096, 8192).rearrange("p (c h) -> p c h", h=128)]
    qbuf = [s3(8192, 8704), s3(8704, 9216)]
    ptile = [s3(9216, 9728), s3(9728, 10240), s3(10240, 10752)]
    rsum = s3(10752, 11264)[0:1, :]
    rbc = s3(11264, 11776)

    C.barrier()

    def chunks_for(qt):
        out = []
        if qt == 0:
            for rr in range(4):
                out.append((rr, 0, True))
        else:
            for rr in range(4):
                out.append((rr, 0, False))
            for rr in range(4):
                out.append((rr, 1, True))
        return out

    pt_done = {}
    qld_done = {0: 0, 1: 0}
    kv_done = {0: 0, 1: 0}
    kvv_done = {0: 0, 1: 0}
    psS_done = {5: 0, 6: 0}
    for h in range(16):
        kvh = h // 4 if layer == 0 else h
        for qt in range(2):
            ch = chunks_for(qt)
            nch = len(ch)
            qsl = qt % 2
            qb = qbuf[qsl]
            C.wait("sync", "t2_" + pre, qld_done[qsl])
            C.dma("sync", qb, g["qd"][h, :, qt * 512:(qt + 1) * 512], "ldq%d_" % qsl + pre)

            def load_kv(ci, sl):
                rr, half, _ = ch[ci]
                ksrc = kg[rr * (KVH * 128) + kvh * 128: rr * (KVH * 128) + (kvh + 1) * 128,
                          half * 512:(half + 1) * 512]
                C.wait("sync", "t2_" + pre, kv_done[sl])
                C.dma("sync", kbuf[sl][:, 0:512], ksrc, "ldk%d_" % sl + pre)
                vsrc = vg[rr * 1024 + half * 512: rr * 1024 + (half + 1) * 512,
                          kvh * 128:(kvh + 1) * 128]
                C.wait("sync", "tav_" + pre, kvv_done[sl])
                C.dma("sync", vbuf[sl][:, 0:4, :],
                      vsrc.rearrange("(c p) h -> p c h", p=128), "ldv%d_" % sl + pre)

            for ci in range(min(2, nch)):
                load_kv(ci, ci % 2)
            pso = PS[2]
            pss = PS[3]
            psb = PS[4]
            C.wait("tensor", "ldq%d_" % qsl + pre, C.cnt["ldq%d_" % qsl + pre])
            njt = nch * 4
            exp_mark = {}

            def emit_score(j):
                ci, kt = j // 4, j % 4
                rr, half, masked = ch[ci]
                sl = ci % 2
                if kt == 0:
                    C.wait("tensor", "ldk%d_" % sl + pre, C.cnt["ldk%d_" % sl + pre])
                kb = kbuf[sl]
                sb = 5 + (j % 2)
                pss_k = PS[sb]
                C.wait("tensor", "s_" + pre, psS_done[sb])
                C.op("tensor", lambda e, kb=kb, kt=kt, qb=qb, pss_k=pss_k: e.matmul(
                    pss_k[:, :], kb[:, kt * 128:(kt + 1) * 128], qb,
                    start=True, stop=True), inc="t2_" + pre)
                if masked:
                    C.wait("vector", "t2_" + pre, C.cnt["t2_" + pre])
                    mi = rr * 4 + kt
                    C.op("vector", lambda e, pss_k=pss_k, mi=mi: e.tensor_tensor(
                        pss_k[:, :], pss_k[:, :], masks[:, mi, :], ALU.add),
                        ins=[masks], inc="v2_" + pre)
                    C.wait("scalar", "v2_" + pre, C.cnt["v2_" + pre])
                else:
                    C.wait("scalar", "t2_" + pre, C.cnt["t2_" + pre])
                pt = ptile[j % 3]
                C.wait("scalar", "tav_" + pre, pt_done.get(j % 3, 0))
                C.op("scalar", lambda e, pt=pt, pss_k=pss_k: e.activation(
                    out=pt, in_=pss_k[:, :], func=AF.Exp),
                    outs=[pt], inc="s_" + pre)
                psS_done[sb] = C.cnt["s_" + pre]
                exp_mark[j] = C.cnt["s_" + pre]

            def emit_av(j):
                ci, kt = j // 4, j % 4
                sl = ci % 2
                vb = vbuf[sl]
                if kt == 0:
                    C.wait("tensor", "ldv%d_" % sl + pre, C.cnt["ldv%d_" % sl + pre])
                pt = ptile[j % 3]
                C.wait("tensor", "s_" + pre, exp_mark[j])
                first = (j == 0)
                last = (j == njt - 1)
                C.op("tensor", lambda e, vb=vb, kt=kt, pt=pt, pso=pso, first=first,
                     last=last: e.matmul(
                    pso[:, :], vb[:, kt, :], pt, start=first, stop=last))
                C.op("tensor", lambda e, pt=pt, pss=pss, first=first, last=last: e.matmul(
                    pss[0:1, :], ones2d[:, 0:1], pt, start=first, stop=last),
                    inc="tav_" + pre)
                pt_done[j % 3] = C.cnt["tav_" + pre]
                if kt == 3:
                    kv_done[sl] = C.cnt["t2_" + pre]
                    kvv_done[sl] = C.cnt["tav_" + pre]
                    if ci + 2 < nch:
                        load_kv(ci + 2, sl)

            emit_score(0)
            for j in range(njt):
                if j + 1 < njt:
                    emit_score(j + 1)
                emit_av(j)
            qld_done[qsl] = C.cnt["t2_" + pre]
            C.wait("vector", "tav_" + pre, C.cnt["tav_" + pre])
            C.op("vector", lambda e, pss=pss: e.reciprocal(rsum, pss[0:1, :]),
                 outs=[rsum], inc="v3_" + pre)
            C.wait("tensor", "v3_" + pre, C.cnt["v3_" + pre])
            C.op("tensor", lambda e, psb=psb: e.matmul(
                psb[:, :], ones2d[0:1, :], rsum, start=True, stop=True),
                inc="t4_" + pre)
            C.wait("scalar", "t4_" + pre, C.cnt["t4_" + pre])
            C.wait("scalar", "v4_" + pre, C.cnt.get("v4_" + pre, 0))
            C.op("scalar", lambda e, psb=psb: e.copy(out=rbc, in_=psb[:, :]),
                 outs=[rbc], inc="s4_" + pre)
            C.wait("vector", "s4_" + pre, C.cnt["s4_" + pre])
            C.op("vector", lambda e, pso=pso, h=h, qt=qt: e.tensor_tensor(
                ao_fm[:, h, qt * 512:(qt + 1) * 512], pso[:, :], rbc, ALU.mult),
                outs=[ao_fm], ins=[rbc], inc="v4_" + pre)
            C.wait("tensor", "v4_" + pre, C.cnt["v4_" + pre])
    C.barrier()


def emit_phase_E(C, nc, env, layer):
    g = env
    s1, s2, s3, sp = g["s1"], g["s2"], g["s3"], g["sp"]
    PS = g["PS"]
    ones2d = g["ones2d"]
    pre = "e"
    ao_fm = s1(0, 16384).rearrange("p (h t) -> p h t", t=1024)
    wt = g["w0o_t"] if layer == 0 else g["w1o_t"]
    bo = g["b0o"] if layer == 0 else g["b1o"]
    x_src = g["x_in"] if layer == 0 else g["x0d"]
    x_dst = g["x0d"] if layer == 0 else g["y_out"]

    brow = s2(8192, 10240)[0:1, :]
    C.dma("sync", brow, bo[:, :], "ldpro_" + pre)
    wstage = [s2(0, 4096).rearrange("p (c o) -> p c o", o=256),
              s2(4096, 8192).rearrange("p (c o) -> p c o", o=256)]
    gt = [s3(0, 2048), s3(2048, 4096)]
    xt = [s3(4096, 6144), s3(6144, 8192)]
    xo = [s3(8192, 10240), s3(10240, 12288)]

    NE = 8
    mm_done = {}
    evac_done = {}
    tile_done = {}

    def load_w(j):
        src_w = wt.rearrange("(c p) o -> p c o", p=128)[:, :, (j % NE) * 256:(j % NE) * 256 + 256]
        C.wait("sync", "t_" + pre, mm_done.get(j - 2, 0))
        C.dma("sync", wstage[j % 2], src_w, "lw%d_" % (j % 2) + pre)

    load_w(0)
    load_w(1)
    C.wait("tensor", "ldpro_" + pre, C.cnt["ldpro_" + pre])
    for t in range(NT):
        sl = t % 2
        if t >= 2:
            C.wait("sync", "v_" + pre, tile_done[t - 2][1])
        C.dma("sync", gt[sl], g["gd"][t, :, :], "lg%d_" % sl + pre)
        C.dma("sync", xt[sl], x_src[t * 128:(t + 1) * 128, :], "lg%d_" % sl + pre)
        for ec in range(NE):
            j = t * NE + ec
            wbuf = wstage[j % 2]
            C.wait("tensor", "lw%d_" % (j % 2) + pre, C.cnt["lw%d_" % (j % 2) + pre])
            bank = j % 2
            ps = PS[bank]
            C.wait("tensor", "v_" + pre, evac_done.get(bank, 0))
            for dc in range(16):
                C.op("tensor", lambda e, ps=ps, dc=dc, t=t, wbuf=wbuf: e.matmul(
                    ps[:, 0:256], ao_fm[:, dc, t * 128:(t + 1) * 128], wbuf[:, dc, :],
                    start=(dc == 0), stop=False))
            C.op("tensor", lambda e, ps=ps, ec=ec: e.matmul(
                ps[:, 0:256], ones2d[0:1, :], brow[:, ec * 256:(ec + 1) * 256],
                start=False, stop=True), inc="t_" + pre)
            mm_done[j] = C.cnt["t_" + pre]
            if j + 2 < NT * NE:
                load_w(j + 2)
            C.wait("vector", "t_" + pre, C.cnt["t_" + pre])
            C.wait("vector", "lg%d_" % sl + pre, C.cnt["lg%d_" % sl + pre])
            if t >= 2 and ec == 0:
                C.wait("vector", "gst%d_" % sl + pre, tile_done[t - 2][0])
            C.op("vector", lambda e, ps=ps, sl=sl, ec=ec: e.tensor_tensor(
                xo[sl][:, ec * 256:(ec + 1) * 256], ps[:, 0:256],
                gt[sl][:, ec * 256:(ec + 1) * 256], ALU.mult),
                outs=[xo[sl]], ins=[gt[sl]], inc="v_" + pre)
            evac_done[bank] = C.cnt["v_" + pre]
            C.op("vector", lambda e, sl=sl, ec=ec: e.tensor_tensor(
                xo[sl][:, ec * 256:(ec + 1) * 256],
                xo[sl][:, ec * 256:(ec + 1) * 256],
                xt[sl][:, ec * 256:(ec + 1) * 256], ALU.add),
                outs=[xo[sl]], ins=[xo[sl], xt[sl]], inc="v_" + pre)
        C.wait("gpsimd", "v_" + pre, C.cnt["v_" + pre])
        C.dma("gpsimd", x_dst[t * 128:(t + 1) * 128, :], xo[sl], "gst%d_" % sl + pre)
        tile_done[t] = (C.cnt["gst%d_" % sl + pre], C.cnt["v_" + pre], C.cnt["v_" + pre])
    C.barrier()



def stage_inputs(inputs):
    f32 = np.float32
    x = np.ascontiguousarray(inputs["L0_attn_x"], dtype=f32)
    tr = lambda a: np.ascontiguousarray(np.asarray(a, dtype=f32).T)
    row = lambda a: np.ascontiguousarray(np.asarray(a, dtype=f32).reshape(1, -1))
    shared = dict(
        w0q_t=tr(inputs["L0_attn_Wq"]), w0k_t=tr(inputs["L0_attn_Wk"]),
        w0v_t=tr(inputs["L0_attn_Wv"]), w0o_t=tr(inputs["L0_attn_Wo"]),
        b0q=row(inputs["L0_attn_bq"]), b0k=row(inputs["L0_attn_bk"]),
        b0v=row(inputs["L0_attn_bv"]), b0o=row(inputs["L0_attn_bo"]),
        gain0=np.ascontiguousarray(
            np.broadcast_to(np.asarray(inputs["L0_attn_q_gain"], f32)[None, :],
                            (128, 16))),
        gs0=np.ascontiguousarray(
            np.asarray(inputs["L0_ffn_gate_scale"], f32).reshape(48, 128).T),
        us0=np.ascontiguousarray(
            np.asarray(inputs["L0_ffn_up_scale"], f32).reshape(48, 128).T),
        ds0bc=np.ascontiguousarray(
            np.broadcast_to(np.asarray(inputs["L0_ffn_down_scale"], f32)[None, :],
                            (128, 2048))),
        w1q_t=tr(inputs["L1_attn_Wq"]), w1k_t=tr(inputs["L1_attn_Wk"]),
        w1v_t=tr(inputs["L1_attn_Wv"]), w1o_t=tr(inputs["L1_attn_Wo"]),
        b1q=row(inputs["L1_attn_bq"]), b1k=row(inputs["L1_attn_bk"]),
        b1v=row(inputs["L1_attn_bv"]), b1o=row(inputs["L1_attn_bo"]),
    )
    g0 = np.ascontiguousarray(np.asarray(inputs["L0_ffn_gate_w"], f32))
    u0 = np.ascontiguousarray(np.asarray(inputs["L0_ffn_up_w"], f32))
    d0 = np.ascontiguousarray(np.asarray(inputs["L0_ffn_down_w"], f32))
    g1 = np.ascontiguousarray(np.asarray(inputs["L1_ffn_gate_w"], f32))
    u1 = np.ascontiguousarray(np.asarray(inputs["L1_ffn_up_w"], f32))
    d1 = np.ascontiguousarray(np.asarray(inputs["L1_ffn_down_w"], f32))

    in_maps = []
    for c in range(NCORES):
        gidx, r = c // RPG, c % RPG
        pos = (r + 4 * np.arange(TLOC)).astype(f32)
        m = dict(shared)
        m["x_in"] = np.ascontiguousarray(x[gidx, r::4, :])
        m["qpos_cols"] = np.ascontiguousarray(pos.reshape(NT, 128).T)
        m["qpos_row"] = np.ascontiguousarray(pos.reshape(1, TLOC))
        m["p0g"] = np.ascontiguousarray(g0[c * 256:(c + 1) * 256, :])
        m["p0u"] = np.ascontiguousarray(u0[c * 256:(c + 1) * 256, :])
        m["p0d"] = np.ascontiguousarray(d0[c * 768:(c + 1) * 768, :])
        m["p1g"] = np.ascontiguousarray(g1[c * 1024:(c + 1) * 1024, :])
        m["p1u"] = np.ascontiguousarray(u1[c * 1024:(c + 1) * 1024, :])
        m["p1d"] = np.ascontiguousarray(d1[c * 256:(c + 1) * 256, :])
        in_maps.append(m)
    return in_maps


_NC_CACHE = {}


def kernel(**inputs):
    key = "full"
    if key not in _NC_CACHE:
        _NC_CACHE[key] = build_program()
    nc, _ = _NC_CACHE[key]
    in_maps = stage_inputs(inputs)
    res = run_bass_kernel_spmd(nc, in_maps, core_ids=list(range(NCORES)))
    out = np.empty((B, S, D), dtype=np.float32)
    for c in range(NCORES):
        gidx, r = c // RPG, c % RPG
        out[gidx, r::4, :] = res.results[c]["y"]
    return out
